# version 08 (final)
# baseline (speedup 1.0000x reference)
import math
import numpy as np
import concourse.bass as bass
import concourse.mybir as mybir
from concourse.bass_utils import run_bass_kernel_spmd

F32 = mybir.dt.float32
BF16 = mybir.dt.bfloat16
AF = mybir.ActivationFunctionType
ALU = mybir.AluOpType

S = 2048
D = 1024
H = 16
DH = 64
DHR = 32
DQK = DH + DHR
DL = 256
NT = 16
ND = 8
NL = 2
QB = 512
NQ = 4
KTL = 128
DHID = 4096
NOC = 32
TFF = 512
NTFF = 4
SM_SCALE = 1.0 / math.sqrt(float(DQK))
CLIP_HI = math.exp(80.0 * SM_SCALE)
CLIP_LO = math.exp(-80.0 * SM_SCALE)
EPS = 1e-5

ENGINES = ("sync", "gpsimd", "vector", "scalar", "tensor")
EXPSPLIT = 0
PAIRED = 1
BARRIER = 0


class Plan:

    def __init__(self, nc):
        self.nc = nc
        self.ops = {e: [] for e in ENGINES}
        self.sems = {}
        self.cnt = {}
        self._es = []

    def sem(self, name):
        g = self.nc.semaphore(name)
        h = g.__enter__()
        self._es.append(g)
        self.sems[name] = h
        self.cnt[name] = 0

    def E(self, eng, fn, waits=(), inc=None, d=1):
        w = tuple(waits)
        tgt = None
        if inc is not None:
            self.cnt[inc] += d
            tgt = self.cnt[inc]
        sems = self.sems

        def g(e, fn=fn, w=w, inc=inc, d=d):
            for (s, v) in w:
                e.wait_ge(sems[s], v)
            ins = fn(e)
            if inc is not None:
                ins.then_inc(sems[inc], d)
        self.ops[eng].append(g)
        return tgt

    def W(self, eng, waits):
        w = tuple(waits)
        sems = self.sems

        def g(e, w=w):
            for (s, v) in w:
                e.wait_ge(sems[s], v)
        self.ops[eng].append(g)

    def replay(self):
        nc = self.nc
        with nc.Block() as block:
            @block.sync
            def _(e):
                for f in self.ops["sync"]:
                    f(e)

            @block.gpsimd
            def _(e):
                for f in self.ops["gpsimd"]:
                    f(e)

            @block.vector
            def _(e):
                for f in self.ops["vector"]:
                    f(e)

            @block.scalar
            def _(e):
                for f in self.ops["scalar"]:
                    f(e)

            @block.tensor
            def _(e):
                for f in self.ops["tensor"]:
                    f(e)
        for g in reversed(self._es):
            g.__exit__(None, None, None)


SB = {}


def _sb(name, off, nbytes, life):
    SB[name] = (off, nbytes, life)
    return off


_sb("xres",   0,      65536, "P")
_sb("stats",  65536,  512,   "P")
_sb("ident",  66048,  256,   "P")
_sb("masks",  66304,  4096,  "P")
_sb("scales", 70400,  768,   "P")
_sb("cos",    71168,  8192,  "A")
_sb("sin",    79360,  8192,  "A")
_sb("x1q",    87552,  8192,  "A")
_sb("x2q",    95744,  8192,  "A")
_sb("ones",   87552,  1024,  "A0")
_sb("wdq",    103936, 4096,  "A")
_sb("wdkv",   108032, 4096,  "A")
_sb("wqr",    112128, 2048,  "A")
_sb("wkr",    114176, 512,   "A")
_sb("xn1t",   114688, 4096,  "A")
_sb("tmpa",   118784, 4096,  "A")
_sb("krx",    122880, 8192,  "A")
_sb("sq",     131072, 4096,  "AB")
_sb("xn1f",   135168, 32768, "AB")
_sb("cq",     167936, 8192,  "AB")
_sb("ckv",    176128, 8192,  "AB")
_sb("r12",    184320, 16384, "AB")
_sb("krr",    200704, 4096,  "AB")
_sb("vall",   71168,  33280, "B")
_sb("wuq",    104448, 4096,  "B")
_sb("wuk",    108544, 4096,  "B")
_sb("wuv",    112640, 4096,  "B")
_sb("qt",     116736, 4096,  "B")
_sb("kt",     120832, 4096,  "B")
_sb("pring",  124928, 4096,  "B")
_sb("stage",  129024, 2048,  "B")
_sb("rb",     204800, 2048,  "B")
_sb("onesf",  206848, 512,   "P")
_sb("wo",     104448, 16384, "BW")
_sb("ytile",  120832, 8192,  "BW")
_sb("xn2t",   71168,  8192,  "C")
_sb("xn2f",   79360,  8192,  "C")
_sb("a_sb",   87552,  32768, "C")
_sb("b_sb",   120320, 32768, "C")
_sb("w12buf", 153088, 8192,  "C")
_sb("hbuf",   161280, 16384, "C")
_sb("w3buf",  177664, 12288, "C")
_sb("w3sb",   189952, 4096,  "C")
_sb("ztile",  194048, 4096,  "C")
_sb("outt",   198144, 4096,  "C")
_sb("sqc",    202240, 4096,  "C")


def _check_map():
    lim = 207 * 1024
    items = [(n, o, b, l) for n, (o, b, l) in SB.items() if b > 0]
    for n, o, b, l in items:
        assert o + b <= lim + 1024, (n, o + b)
    phases = {"A": ("P", "A", "AB", "ABC"),
              "B": ("P", "AB", "ABC", "B"),
              "BW": ("P", "ABC", "BW", "AB"),
              "C": ("P", "ABC", "C")}
    for ph, lives in phases.items():
        acc = []
        for n, o, b, l in items:
            if l in lives:
                acc.append((o, o + b, n))
        acc.sort()
        for i in range(len(acc) - 1):
            assert acc[i][1] <= acc[i + 1][0], (ph, acc[i], acc[i + 1])


_check_map()


def build_program(dbg=0, upto="C", no_ffncast=0):
    nc = bass.Bass("TRN2", target_bir_lowering=False)
    p = Plan(nc)
    t = {}

    def din(name, shape, dtype=F32):
        t[name] = nc.dram_tensor(name, list(shape), dtype, kind="ExternalInput").ap()
    din("x", (S, D))
    din("wdq", (D, DL)); din("wdkv", (D, DL))
    din("wuq", (DL, D)); din("wuk", (DL, D)); din("wuv", (DL, D))
    din("wqr", (DL, H * DHR))
    din("wkr", (D, DHR))
    din("wo", (D, D))
    din("costab", (128, S)); din("sintab", (128, S))
    din("w1t", (NOC, D, 128)); din("w2t", (NOC, D, 128))
    din("h1t", (NOC, DHID, 128)); din("h2t", (NOC, DHID, 128))
    din("w3t", (NOC, 128, D))
    din("w1s", (128, NOC)); din("w2s", (128, NOC)); din("h1s", (128, NOC)); din("h2s", (128, NOC))
    din("w3s", (128, ND))
    din("w3s_row", (1, D))
    t["y"] = nc.dram_tensor("y", [S, D], F32, kind="ExternalOutput").ap()
    for nm, shp in (("uqb", (DL, D)), ("ukb", (DL, D)),
                    ("uvb", (DL, D)), ("wob", (D, D))):
        t[nm] = nc.dram_tensor(nm, list(shp), BF16, kind="Internal").ap()
    for nm, shp in (("w1b", (NOC, D, 128)), ("w2b", (NOC, D, 128)),
                    ("h1b", (NOC, DHID, 128)), ("h2b", (NOC, DHID, 128)),
                    ("w3b", (NOC, 128, D))):
        t[nm] = nc.dram_tensor(nm, list(shp), BF16, kind="Internal").ap()

    MAP_BYTES = 206336
    arena = nc.alloc_sbuf_tensor("arena", [128, MAP_BYTES], mybir.dt.uint8)
    SB_BASE = nc.lookup_mloc(arena).addr

    def sb(name, shape, dtype, key=None, extra=0):
        off, nbytes, _ = SB[key or name]
        h = nc.alloc_sbuf_tensor_at(name, list(shape), dtype, offset=SB_BASE + off + extra)
        t[name] = h
        return h
    sb("xres", (128, NT * D), F32)
    sb("stats", (128, 128), F32)
    sb("ident", (128, 128), BF16)
    sb("masks", (128, 4 * 512), BF16)
    sb("w1s_sb", (128, 32), F32, key="scales", extra=0)
    sb("w2s_sb", (128, 32), F32, key="scales", extra=128)
    sb("h1s_sb", (128, 32), F32, key="scales", extra=256)
    sb("h2s_sb", (128, 32), F32, key="scales", extra=384)
    sb("w3s_sb", (128, 8), F32, key="scales", extra=512)
    sb("consts", (128, 8), F32, key="scales", extra=576)
    sb("cos", (128, S), F32); sb("sin", (128, S), F32)
    sb("x1q", (128, 2 * S), BF16); sb("x2q", (128, 2 * S), BF16)
    sb("ones", (128, 512), BF16)
    sb("wdq_sb", (128, ND * DL), BF16, key="wdq")
    sb("wdkv_sb", (128, ND * DL), BF16, key="wdkv")
    sb("wqr_sb", (128, NL * 512), BF16, key="wqr")
    sb("wkr_sb", (128, ND * DHR), BF16, key="wkr")
    sb("xn1t", (128, 2 * D), BF16)
    sb("tmpa", (128, 2 * 512), F32)
    sb("krx", (16, 2 * S), BF16)
    sb("sq", (128, D), F32)
    sb("xn1f", (128, ND * S), BF16)
    sb("cq", (128, NL * S), BF16)
    sb("ckv", (128, NL * S), BF16)
    sb("r12", (128, 2 * 2 * S), BF16)
    sb("krr", (96, S), BF16)
    sb("vall", (128, NT * 1040), BF16)
    sb("wuq_sb", (128, NL * D), BF16, key="wuq")
    sb("wuk_sb", (128, NL * D), BF16, key="wuk")
    sb("wuv_sb", (128, NL * D), BF16, key="wuv")
    sb("qt", (128, S), BF16)
    sb("kt", (128, S), BF16)
    sb("pring", (128, 4 * 512), BF16)
    sb("stage", (128, 512), F32)
    sb("wo_sb", (128, ND * D), BF16, key="wo")
    sb("ytile", (128, 2 * D), F32)
    sb("rb", (64, 512), F32)
    sb("onesf", (65, 128), F32)
    sb("xn2t", (128, 4 * D), BF16)
    sb("xn2f", (128, ND * TFF), BF16)
    sb("a_sb", (128, NOC * TFF), BF16)
    sb("b_sb", (128, NOC * TFF), BF16)
    sb("w12buf", (128, 4 * 8 * 128), BF16)
    sb("hbuf", (128, 2 * NOC * 128), BF16)
    sb("w3buf", (128, 8 * D), BF16)
    sb("w3sb", (128, D), F32)
    sb("ztile", (128, D), F32)
    sb("outt", (128, D), F32)
    sb("sqc", (128, D), F32)
    t["w3srow"] = nc.alloc_sbuf_tensor_at("w3srow", [65, D], F32, offset=SB_BASE + SB["ztile"][0])

    pg = []
    def ps(name, shape, dtype):
        g = nc.psum_tensor(name, list(shape), dtype)
        h = g.__enter__(); pg.append(g); t[name] = h
        return h
    ps("ps_s0", (128, 1024), F32); ps("ps_s1", (128, 1024), F32)
    ps("ps_c", (128, 512), F32); ps("ps_d", (128, 512), F32)
    ps("ps_e", (128, 512), F32); ps("ps_f", (128, 512), F32)

    for s in ("LDX", "LD", "RP", "GW", "GF", "SD", "TE", "VE", "SC", "GP", "ST", "TEA",
              "W12A", "W12B", "HBA", "HBB", "W30", "W31", "W32", "W33", "W34", "W35"):
        p.sem(s)

    dbg_outs = []
    def dump(name, src_fn, shape, dtype, waits):
        if not dbg or (dbg == 3 and name not in ("x1", "mlo", "mbs", "attn")):
            return
        ap = nc.dram_tensor("dbg_" + name, list(shape), dtype, kind="ExternalOutput").ap()
        p.E("sync", lambda e, ap=ap, s=src_fn: e.dma_start(out=ap, in_=s()),
            waits=waits, inc="ST", d=16)
        dbg_outs.append(name)

    ld_x = p.E("sync", lambda e: e.dma_start(
        out=t["xres"][:, :].rearrange("p (n m) -> p n m", m=D),
        in_=t["x"].rearrange("(n p) m -> p n m", p=128)), inc="LDX", d=16)
    ld_cos = p.E("sync", lambda e: e.dma_start(out=t["cos"][:, :], in_=t["costab"][:, :]), inc="LD", d=16)
    ld_sin = p.E("sync", lambda e: e.dma_start(out=t["sin"][:, :], in_=t["sintab"][:, :]), inc="LD", d=16)
    for nm, dst in (("w1s", "w1s_sb"), ("w2s", "w2s_sb"), ("h1s", "h1s_sb"), ("h2s", "h2s_sb"),
                    ("w3s", "w3s_sb")):
        p.E("sync", lambda e, nm=nm, dst=dst: e.dma_start(
            out=t[dst][:, :], in_=t[nm][:, :]), inc="LD", d=16)
    ld_misc = p.cnt["LD"]

    p.E("gpsimd", lambda e: e.memset(t["consts"][:, 0:1], EPS), inc="GP")
    p.E("gpsimd", lambda e: e.memset(t["consts"][:, 1:2], SM_SCALE), inc="GP")
    p.E("gpsimd", lambda e: e.memset(t["consts"][:, 2:3], -1.0), inc="GP")
    p.E("gpsimd", lambda e: e.memset(t["consts"][:, 3:4], -0.5), inc="GP")
    g_consts = p.cnt["GP"]
    p.E("gpsimd", lambda e: e.memset(t["onesf"][64:65, :], 1.0), inc="GP")
    g_onesf = p.cnt["GP"]
    g1 = p.E("gpsimd", lambda e: e.memset(t["ones"][:, :], 1.0), inc="GP")
    p.W("gpsimd", [("GP", g1)])
    g_id = p.E("gpsimd", lambda e: e.affine_select(
        t["ident"][:, :], t["ones"][:, 0:128], [[1, 128]], ALU.is_equal, 0.0,
        base=0, channel_multiplier=-1), inc="GP")
    gz = p.E("gpsimd", lambda e: e.memset(t["ones"][:, :], 0.0), inc="GP")
    p.W("gpsimd", [("GP", gz)])
    for j in range(4):
        p.E("gpsimd", lambda e, j=j: e.affine_select(
            t["masks"][:, j * 512:(j + 1) * 512], t["ones"][:, :], [[1, 512]],
            ALU.is_ge, -1.0e5, base=-j * 128, channel_multiplier=-1), inc="GP")
    g_masks = p.cnt["GP"]

    def cast_load(dst, src, rearr, waits=()):
        return p.E("gpsimd", lambda e: e.dma_start(
            out=t[dst][:, :].rearrange("p (n m) -> p n m", m=rearr),
            in_=t[src].rearrange("(n p) m -> p n m", p=128)), waits=waits, inc="GW", d=16)
    cast_load("wdq_sb", "wdq", DL)
    cast_load("wdkv_sb", "wdkv", DL)
    cast_load("wqr_sb", "wqr", 512)
    cast_load("wkr_sb", "wkr", DHR)
    g_wdq = g_wdkv = g_wqr = g_wkr = p.cnt["GW"]
    p.W("gpsimd", [("GW", p.cnt["GW"])])
    for a, b in (("wuq", "uqb"), ("wuk", "ukb"), ("wuv", "uvb"), ("wo", "wob")):
        p.E("gpsimd", lambda e, a=a, b=b: e.dma_start(
            out=t[b].rearrange("a b -> (a b)"), in_=t[a].rearrange("a b -> (a b)")),
            inc="GW", d=16)
    g_uqkv_cast = p.cnt["GW"]
    flood_ld_gate = p.cnt["LD"] + 48
    for src, dst in (("w1t", "w1b"), ("w2t", "w2b"), ("w3t", "w3b")):
        for oc in range(NOC):
            p.E("gpsimd", lambda e, src=src, dst=dst, oc=oc: e.dma_start(
                out=t[dst][oc].rearrange("a b -> (a b)"),
                in_=t[src][oc].rearrange("a b -> (a b)")),
                inc="GF", d=16)

    st = t["stats"]
    sc_sq = []
    ve_sum = []
    for n in range(NT):
        xt = t["xres"][:, n * D:(n + 1) * D]
        sc_sq.append(p.E("vector", lambda e, xt=xt, n=n: e.scalar_tensor_tensor(
            t["sq"][:, :], xt, 1.0, xt, ALU.mult, ALU.mult,
            accum_out=st[:, 16 + n:17 + n]),
            waits=[("LDX", ld_x)] if n == 0 else (), inc="VE"))
        ve_sum.append(p.E("vector", lambda e, xt=xt, n=n: e.reduce_sum(
            st[:, n:n + 1], xt, axis=mybir.AxisListType.X), inc="VE"))
    p.W("vector", [("VE", ve_sum[-1])])
    p.E("vector", lambda e: e.tensor_scalar_mul(st[:, 32:48], st[:, 0:16], 1.0 / D), inc="VE")
    v_msq = p.E("vector", lambda e: e.tensor_scalar_mul(st[:, 64:80], st[:, 16:32], 1.0 / D), inc="VE")
    s_msq = p.E("scalar", lambda e: e.activation(st[:, 80:96], st[:, 32:48], AF.Square),
                waits=[("VE", v_msq)], inc="SC")
    v_var = p.E("vector", lambda e: e.tensor_sub(st[:, 96:112], st[:, 64:80], st[:, 80:96]),
                waits=[("SC", s_msq)], inc="VE")
    s_lnv = p.E("scalar", lambda e: e.activation(st[:, 112:128], st[:, 96:112], AF.Ln,
                                                 bias=t["consts"][:, 0:1]),
                waits=[("VE", v_var), ("GP", g_consts)], inc="SC")
    p.W("scalar", [("SC", s_lnv)])
    s_inv = p.E("scalar", lambda e: e.activation(st[:, 48:64], st[:, 112:128], AF.Exp,
                                                 scale=t["consts"][:, 3:4]), inc="SC")
    v_inv = s_inv

    ev_done = [None, None]
    for n in range(NT):
        pp = n % 2
        xt = t["xres"][:, n * D:(n + 1) * D]
        xn = t["xn1t"][:, pp * D:(pp + 1) * D]
        w = [("SC", v_inv)]
        v_app = p.E("vector", lambda e, xt=xt, xn=xn, n=n: e.tensor_scalar(
            xn, xt, st[:, 32 + n:33 + n], st[:, 48 + n:49 + n], ALU.subtract, ALU.mult),
            waits=w, inc="VE")
        for half in range(2):
            pst = t["ps_e"] if half == 0 else t["ps_f"]
            w = [("VE", v_app), ("GP", g_id)]
            if ev_done[half] is not None:
                w.append(("VE", ev_done[half]))
            for k in range(4):
                d = half * 4 + k
                p.E("tensor", lambda e, pst=pst, xn=xn, d=d, k=k: e.matmul(
                    pst[:, k * 128:(k + 1) * 128], xn[:, d * 128:(d + 1) * 128],
                    t["ident"][:, :], start=True, stop=True),
                    waits=(w if k == 0 else ()), inc=("TE" if k == 3 else None))
            trd = p.cnt["TE"]
            ev_done[half] = p.E("vector", lambda e, pst=pst, n=n, half=half: e.tensor_copy(
                bass.AP(t["xn1f"], half * 4 * S + n * 128, [[NT * D, 128], [S, 4], [1, 128]]),
                pst[:, :].rearrange("p (d m) -> p d m", m=128)),
                waits=[("TE", trd)], inc="VE")
    v_xn1f = p.cnt["VE"]

    def xn1f_sl(d, q):
        return t["xn1f"][:, d * S + q * QB: d * S + q * QB + QB]

    pb_free = {0: ev_done[0], 1: ev_done[1]}
    pp = 0

    def proj(dst, dstoff, wsb, wslice_fn, mbase, msize, q, extra_w=()):
        nonlocal pp
        pse = t["ps_e"] if pp == 0 else t["ps_f"]
        w = [("VE", v_xn1f)] + list(extra_w)
        if pb_free[pp] is not None:
            w.append(("VE", pb_free[pp]))
        for d in range(ND):
            p.E("tensor", lambda e, pse=pse, d=d, q=q, wf=wslice_fn: e.matmul(
                pse[mbase:mbase + msize, :], wf(d), xn1f_sl(d, q),
                start=(d == 0), stop=(d == ND - 1)),
                waits=(w if d == 0 else ()), inc=("TE" if d == ND - 1 else None))
        mm = p.cnt["TE"]
        ev = p.E("vector", lambda e, pse=pse, dst=dst, o=dstoff: e.tensor_copy(
            t[dst][mbase:mbase + msize, o:o + QB], pse[mbase:mbase + msize, :]),
            waits=[("TE", mm)], inc="VE")
        pb_free[pp] = ev
        pp ^= 1
        return ev

    cq_done = []
    for l in range(NL):
        for q in range(NQ):
            cq_done.append(proj("cq", l * S + q * QB,
                                "wdq_sb", lambda d, l=l: t["wdq_sb"][:, d * DL + l * 128: d * DL + (l + 1) * 128],
                                0, 128, q, extra_w=[("GW", g_wdq)]))
    ckv_done = []
    for l in range(NL):
        for q in range(NQ):
            ckv_done.append(proj("ckv", l * S + q * QB,
                                 "wdkv_sb", lambda d, l=l: t["wdkv_sb"][:, d * DL + l * 128: d * DL + (l + 1) * 128],
                                 0, 128, q, extra_w=[("GW", g_wdkv)]))
    kr_done = []
    for half in range(2):
        for q in range(NQ):
            kr_done.append(proj("krx", half * S + q * QB,
                                "wkr_sb", lambda d, h=half: t["wkr_sb"][:, d * DHR + h * 16: d * DHR + h * 16 + 16],
                                0, 16, q, extra_w=[("GW", g_wkr)]))
    qr_done = []
    for half, dst in ((0, "x1q"), (1, "x2q")):
        for c in range(2):
            for q in range(NQ):
                nonloc_pp = pp
                pse = t["ps_e"] if pp == 0 else t["ps_f"]
                w = [("GW", g_wqr)]
                if pb_free[pp] is not None:
                    w.append(("VE", pb_free[pp]))
                for l in range(NL):
                    p.E("tensor", lambda e, pse=pse, l=l, c=c, q=q, h=half: e.matmul(
                        pse[:, :], t["wqr_sb"][:, l * 512 + h * 256 + c * 128: l * 512 + h * 256 + (c + 1) * 128],
                        t["cq"][:, l * S + q * QB: l * S + q * QB + QB],
                        start=(l == 0), stop=(l == NL - 1)),
                        waits=(w + [("VE", cq_done[-1])] if l == 0 else ()),
                        inc=("TE" if l == NL - 1 else None))
                mm = p.cnt["TE"]
                ev = p.E("vector", lambda e, pse=pse, dst=dst, c=c, q=q: e.tensor_copy(
                    t[dst][:, c * S + q * QB: c * S + q * QB + QB], pse[:, :]),
                    waits=[("TE", mm)], inc="VE")
                pb_free[pp] = ev
                qr_done.append(ev)
                pp ^= 1

    def rope_block(x1, x2, co, si, r1, r2, w0, part=128):
        tA = t["tmpa"][0:part, 0:512]
        tB = t["tmpa"][0:part, 512:1024]
        tC = t["sq"].ap() if False else t["sq"]
        tC = t["sq"][0:part, 0:512]
        tD = t["sq"][0:part, 512:1024]
        p.E("vector", lambda e: e.tensor_mul(tA, x1, co), waits=w0, inc="VE")
        p.E("vector", lambda e: e.tensor_mul(tB, x1, si), inc="VE")
        p.E("vector", lambda e: e.tensor_mul(tC, x2, si), inc="VE")
        g = p.E("vector", lambda e: e.tensor_mul(tD, x2, co), inc="VE")
        p.W("vector", [("VE", g)])
        p.E("vector", lambda e: e.tensor_sub(r1, tA, tC), inc="VE")
        return p.E("vector", lambda e: e.tensor_add(r2, tB, tD), inc="VE")

    rope_q_done = []
    for c in range(2):
        for q in range(NQ):
            x1 = t["x1q"][:, c * S + q * QB: c * S + q * QB + QB]
            x2 = t["x2q"][:, c * S + q * QB: c * S + q * QB + QB]
            co = t["cos"][:, q * QB:(q + 1) * QB]
            si = t["sin"][:, q * QB:(q + 1) * QB]
            r1 = t["r12"][:, c * S + q * QB: c * S + q * QB + QB]
            r2 = t["r12"][:, 2 * S + c * S + q * QB: 2 * S + c * S + q * QB + QB]
            w = [("VE", qr_done[-1]), ("LD", ld_misc)] if (c == 0 and q == 0) else ()
            rope_q_done.append(rope_block(x1, x2, co, si, r1, r2, w))

    kr2tmp = t["x1q"]
    rope_k_parts = []
    for q in range(NQ):
        x1 = t["krx"][0:16, q * QB:(q + 1) * QB]
        x2 = t["krx"][0:16, S + q * QB: S + q * QB + QB]
        co = t["cos"][0:16, q * QB:(q + 1) * QB]
        si = t["sin"][0:16, q * QB:(q + 1) * QB]
        r1 = kr2tmp[0:16, q * QB:(q + 1) * QB]
        r2 = kr2tmp[0:16, S + q * QB: S + (q + 1) * QB]
        w = [("VE", rope_q_done[-1]), ("VE", kr_done[-1])] if q == 0 else ()
        rope_k_parts.append(rope_block(x1, x2, co, si, r1, r2, w, part=16))
    v_ropek = rope_k_parts[-1]
    p.E("sync", lambda e: e.dma_start(out=t["krr"][64:80, :], in_=kr2tmp[0:16, 0:S]),
        waits=[("VE", v_ropek)], inc="ST", d=16)
    d_krr = p.E("sync", lambda e: e.dma_start(out=t["krr"][80:96, :], in_=kr2tmp[0:16, S:2 * S]),
                inc="ST", d=16)

    if dbg:
        dump("xn1f", lambda: t["xn1f"][:, :], (128, ND * S), BF16, [("VE", v_xn1f)])
        dump("cq", lambda: t["cq"][:, :], (128, NL * S), BF16, [("VE", cq_done[-1])])
        dump("ckv", lambda: t["ckv"][:, :], (128, NL * S), BF16, [("VE", ckv_done[-1])])
        dump("r12", lambda: t["r12"][:, :], (128, 4 * S), BF16, [("VE", rope_q_done[-1])])
        dump("krr", lambda: t["krr"][64:96, :], (32, S), BF16, [("ST", d_krr)])

    te_projA = p.cnt["TE"]
    if upto in ("B", "C"):
        wB = [("TE", te_projA), ("VE", v_ropek), ("ST", d_krr)]
        for a, b in (("wuq_sb", "uqb"), ("wuk_sb", "ukb"), ("wuv_sb", "uvb")):
            p.E("sync", lambda e, a=a, b=b: e.dma_start(
                out=t[a][:, :].rearrange("p (n m) -> p n m", m=D),
                in_=t[b].rearrange("(n p) m -> p n m", p=128)),
                waits=list(wB) + [("GW", g_uqkv_cast)], inc="LD", d=16)
            wB = []
        g_wuqkv = ("LD", p.cnt["LD"])
        masklo = nc.alloc_sbuf_tensor_at("masklo", [128, 4 * 512], BF16,
                                         offset=SB_BASE + SB["sq"][0])
        t["masklo"] = masklo
        glo = p.E("gpsimd", lambda e: e.memset(masklo[:, :], CLIP_LO),
                  waits=[("VE", v_ropek), ("ST", d_krr)], inc="GP")
        p.W("gpsimd", [("GP", glo)])
        for j in range(4):
            p.E("gpsimd", lambda e, j=j: e.affine_select(
                masklo[:, j * 512:(j + 1) * 512], masklo[:, j * 512:(j + 1) * 512],
                [[1, 512]], ALU.is_ge, 0.0, base=-j * 128, channel_multiplier=-1), inc="GP")
        g_masklo = p.cnt["GP"]
        g_vones = p.E("gpsimd", lambda e: e.memset(
            bass.AP(t["vall"], 64, [[NT * 1040, 128], [1040, NT], [65, H], [1, 1]]), 1.0),
            waits=list(wB), inc="GP")
        for srcn, dstn in (("h1t", "h1b"), ("h2t", "h2b")):
            for oc in range(NOC):
                wv = [("LD", flood_ld_gate)] if (srcn == "h1t" and oc == 0) else ()
                p.E("gpsimd", lambda e, srcn=srcn, dstn=dstn, oc=oc: e.dma_start(
                    out=t[dstn][oc].rearrange("a b -> (a b)"),
                    in_=t[srcn][oc].rearrange("a b -> (a b)")),
                    waits=wv, inc="GF", d=16)
        g_ffncast = p.cnt["GF"]

        vb_free = {0: None, 1: None}
        vpp = 0
        v_vall = None
        for j in range(NT):
            for half in range(2):
                pse = t["ps_e"] if vpp == 0 else t["ps_f"]
                w = [g_wuqkv, ("VE", ckv_done[-1]), ("TE", te_projA)]
                if vb_free[vpp] is not None:
                    w.append(vb_free[vpp])
                for l in range(NL):
                    p.E("tensor", lambda e, pse=pse, l=l, j=j, hf=half: e.matmul(
                        pse[:, :], t["ckv"][:, l * S + j * 128: l * S + (j + 1) * 128],
                        t["wuv_sb"][:, l * D + hf * 512: l * D + hf * 512 + 512],
                        start=(l == 0), stop=(l == NL - 1)),
                        waits=(w if l == 0 else ()), inc=("TE" if l == NL - 1 else None))
                mm = p.cnt["TE"]
                ev = p.E("vector", lambda e, pse=pse, j=j, hf=half: e.tensor_copy(
                    bass.AP(t["vall"], j * 1040 + hf * 520, [[NT * 1040, 128], [65, 8], [1, 64]]),
                    pse[:, :].rearrange("p (h m) -> p h m", m=64)),
                    waits=[("TE", mm)], inc="VE")
                vb_free[vpp] = ("VE", ev)
                vpp ^= 1
        v_vall = p.cnt["VE"]
        if dbg >= 2:
            dump("vallA", lambda: t["vall"][:, 0:2080], (128, 2080), BF16, [("VE", v_vall), ("GP", g_vones)])

        if BARRIER:
            barv = {s: p.cnt[s] for s in ("TE", "VE", "SC", "GP", "LD", "ST", "GW", "LDX")}
            engs = {1: ("tensor", "vector", "scalar", "gpsimd", "sync"),
                    2: ("tensor",), 3: ("sync",), 4: ("vector",), 5: ("scalar",), 6: ("gpsimd",)}[BARRIER]
            for eng in engs:
                p.W(eng, [(s, v) for s, v in barv.items()])

        qcg = 0
        gpi = 0
        ve_stage = {}
        gp_div = {}
        sc_exp = {}
        te_last_score = 0
        for h in range(H):
            for q in range(NQ):
                pse = t["ps_e"] if vpp == 0 else t["ps_f"]
                w = [g_wuqkv, ("VE", cq_done[-1])]
                if vb_free[vpp] is not None:
                    w.append(vb_free[vpp])
                for l in range(NL):
                    p.E("tensor", lambda e, pse=pse, l=l, q=q, h=h: e.matmul(
                        pse[0:64, :], t["wuq_sb"][:, l * D + h * DH: l * D + h * DH + DH],
                        t["cq"][:, l * S + q * QB: l * S + q * QB + QB],
                        start=(l == 0), stop=(l == NL - 1)),
                        waits=(w if l == 0 else ()), inc=("TE" if l == NL - 1 else None))
                mm = p.cnt["TE"]
                ev = p.E("vector", lambda e, pse=pse, q=q: e.tensor_copy(
                    t["qt"][0:64, q * QB:(q + 1) * QB], pse[0:64, :]),
                    waits=[("TE", mm)], inc="VE")
                vb_free[vpp] = ("VE", ev)
                vpp ^= 1
            for q in range(NQ):
                pse = t["ps_e"] if vpp == 0 else t["ps_f"]
                w = [g_wuqkv, ("VE", ckv_done[-1])]
                if vb_free[vpp] is not None:
                    w.append(vb_free[vpp])
                for l in range(NL):
                    p.E("tensor", lambda e, pse=pse, l=l, q=q, h=h: e.matmul(
                        pse[0:64, :], t["wuk_sb"][:, l * D + h * DH: l * D + h * DH + DH],
                        t["ckv"][:, l * S + q * QB: l * S + q * QB + QB],
                        start=(l == 0), stop=(l == NL - 1)),
                        waits=(w if l == 0 else ()), inc=("TE" if l == NL - 1 else None))
                mm = p.cnt["TE"]
                ev = p.E("vector", lambda e, pse=pse, q=q: e.tensor_copy(
                    t["kt"][0:64, q * QB:(q + 1) * QB], pse[0:64, :]),
                    waits=[("TE", mm)], inc="VE")
                vb_free[vpp] = ("VE", ev)
                vpp ^= 1
            v_ktrope = p.E("vector", lambda e: e.tensor_copy(
                t["kt"][64:96, :], t["krr"][64:96, :]),
                waits=[("TE", te_last_score), ("ST", d_krr)], inc="VE")
            rw = [("TE", te_last_score), ("VE", rope_q_done[-1])]
            g_q1 = p.E("sync", lambda e, h=h: e.dma_start(
                out=t["qt"][64:80, :],
                in_=t["r12"][(h % 8) * 16:(h % 8) * 16 + 16, (h // 8) * S:(h // 8) * S + S]),
                waits=rw, inc="RP", d=16)
            g_q2 = p.E("sync", lambda e, h=h: e.dma_start(
                out=t["qt"][80:96, :],
                in_=t["r12"][(h % 8) * 16:(h % 8) * 16 + 16, 2 * S + (h // 8) * S: 2 * S + (h // 8) * S + S]),
                inc="RP", d=16)
            g_rope = p.cnt["RP"]
            v_build = p.cnt["VE"]

            for Q in range(NQ):
                nk = 4 * Q + 4
                npair = nk // 2
                g0 = gpi
                pso = t["ps_c"] if qcg % 2 == 0 else t["ps_d"]
                ve_stt = {}
                pv_t = {}
                if PAIRED:
                    for pr in range(npair + 1):
                        if pr < npair:
                            pss = t["ps_s0"] if gpi % 2 == 0 else t["ps_s1"]
                            for k2 in range(2):
                                j = 2 * pr + k2
                                diag = j >= nk - 4
                                w = []
                                if j == 0:
                                    w += [("VE", v_build), ("RP", g_rope), ("GP", g_vones)]
                                    if qcg >= 2 and (qcg - 2) in ve_stage:
                                        w.append(("VE", ve_stage[qcg - 2]))
                                if k2 == 0 and gpi >= 2:
                                    w.append(("SC", sc_exp[gpi - 2]))
                                p.E("tensor", lambda e, pss=pss, j=j, Q=Q, diag=diag, k2=k2: e.matmul(
                                    pss[:, k2 * 512:(k2 + 1) * 512],
                                    t["kt"][0:96, j * 128:(j + 1) * 128],
                                    t["qt"][0:96, Q * QB:(Q + 1) * QB],
                                    start=True, stop=(not diag)),
                                    waits=w, inc=(None if diag else "TE"))
                                if diag:
                                    rel = j - (nk - 4)
                                    p.E("tensor", lambda e, pss=pss, rel=rel, k2=k2: e.matmul(
                                        pss[:, k2 * 512:(k2 + 1) * 512], t["ident"][:, :],
                                        t["masks"][:, rel * 512:(rel + 1) * 512],
                                        start=False, stop=True),
                                        waits=[("GP", g_masks)] if (qcg == 0 and rel == 0) else (),
                                        inc="TE")
                                else:
                                    pass
                            te_pair = p.cnt["TE"]
                            sw = [("TE", te_pair)]
                            if qcg == 0 and pr == 0:
                                sw.append(("GP", g_consts))
                            if EXPSPLIT:
                                p.E("scalar", lambda e, gpi=gpi, pss=pss: e.activation(
                                    t["pring"][:, (gpi % 2) * 1024:(gpi % 2) * 1024 + 512],
                                    pss[:, 0:512], AF.Exp, scale=t["consts"][:, 1:2]),
                                    waits=sw, inc="SC")
                                sc_exp[gpi] = p.E("scalar", lambda e, gpi=gpi, pss=pss: e.activation(
                                    t["pring"][:, (gpi % 2) * 1024 + 512:(gpi % 2) * 1024 + 1024],
                                    pss[:, 512:1024], AF.Exp, scale=t["consts"][:, 1:2]),
                                    inc="SC")
                            else:
                                sc_exp[gpi] = p.E("scalar", lambda e, gpi=gpi, pss=pss: e.activation(
                                    t["pring"][:, (gpi % 2) * 1024:(gpi % 2) * 1024 + 1024],
                                    pss[:, :], AF.Exp, scale=t["consts"][:, 1:2]),
                                    waits=sw, inc="SC")
                            vw = [("SC", sc_exp[gpi])]
                            if qcg == 0 and pr == 0:
                                vw.append(("GP", g_masklo))
                            diagpair = (2 * pr) >= nk - 4
                            if diagpair:
                                rel = 2 * pr - (nk - 4)
                                ve_stt[pr] = p.E("vector", lambda e, gpi=gpi, rel=rel: e.scalar_tensor_tensor(
                                    t["pring"][:, (gpi % 2) * 1024:(gpi % 2) * 1024 + 1024],
                                    t["pring"][:, (gpi % 2) * 1024:(gpi % 2) * 1024 + 1024],
                                    CLIP_HI, masklo[:, rel * 512:(rel + 2) * 512],
                                    ALU.min, ALU.max),
                                    waits=vw, inc="VE")
                            else:
                                ve_stt[pr] = p.E("vector", lambda e, gpi=gpi: e.tensor_scalar(
                                    t["pring"][:, (gpi % 2) * 1024:(gpi % 2) * 1024 + 1024],
                                    t["pring"][:, (gpi % 2) * 1024:(gpi % 2) * 1024 + 1024],
                                    CLIP_HI, CLIP_LO, ALU.min, ALU.max),
                                    waits=vw, inc="VE")
                            gpi += 1
                        if pr >= 1:
                            for k2 in range(2):
                                jj = 2 * (pr - 1) + k2
                                slotb = (g0 + pr - 1) % 2
                                p.E("tensor", lambda e, jj=jj, h=h, pso=pso, k2=k2, slotb=slotb: e.matmul(
                                    pso[0:65, :], t["vall"][:, jj * 1040 + h * 65: jj * 1040 + h * 65 + 65],
                                    t["pring"][:, slotb * 1024 + k2 * 512: slotb * 1024 + k2 * 512 + 512],
                                    start=(jj == 0), stop=(jj == nk - 1)),
                                    waits=([("VE", ve_stt[pr - 1])] if k2 == 0 else ()), inc="TE")
                                pv_t[jj] = p.cnt["TE"]
                else:
                    te_score = {}
                    for j in range(nk + 1):
                        if j < nk:
                            diag = j >= nk - 4
                            pss = t["ps_s0"][:, 0:512] if j % 2 == 0 else t["ps_s1"][:, 0:512]
                            w = []
                            if j == 0:
                                w += [("VE", v_build), ("RP", g_rope), ("GP", g_vones)]
                                if qcg >= 2 and (qcg - 2) in ve_stage:
                                    w.append(("VE", ve_stage[qcg - 2]))
                            if k := (gpi >= 2 and j == j):
                                pass
                            if gpi >= 2:
                                w.append(("SC", sc_exp[gpi - 2]))
                            p.E("tensor", lambda e, pss=pss, j=j, Q=Q, diag=diag: e.matmul(
                                pss, t["kt"][0:96, j * 128:(j + 1) * 128],
                                t["qt"][0:96, Q * QB:(Q + 1) * QB],
                                start=True, stop=(not diag)),
                                waits=w, inc=(None if diag else "TE"))
                            if diag:
                                rel = j - (nk - 4)
                                p.E("tensor", lambda e, pss=pss, rel=rel: e.matmul(
                                    pss, t["ident"][:, :],
                                    t["masks"][:, rel * 512:(rel + 1) * 512],
                                    start=False, stop=True),
                                    waits=[("GP", g_masks)] if (qcg == 0 and rel == 0) else (),
                                    inc="TE")
                            te_score[j] = p.cnt["TE"]
                            sw = [("TE", te_score[j])]
                            if qcg == 0 and j == 0:
                                sw.append(("GP", g_consts))
                            sc_exp[gpi] = p.E("scalar", lambda e, gpi=gpi, pss=pss: e.activation(
                                t["pring"][:, (gpi % 4) * 512:(gpi % 4) * 512 + 512],
                                pss, AF.Exp, scale=t["consts"][:, 1:2]),
                                waits=sw, inc="SC")
                            vw = [("SC", sc_exp[gpi])]
                            if qcg == 0 and j == 0:
                                vw.append(("GP", g_masklo))
                            if diag:
                                rel = j - (nk - 4)
                                ve_stt[j] = p.E("vector", lambda e, gpi=gpi, rel=rel: e.scalar_tensor_tensor(
                                    t["pring"][:, (gpi % 4) * 512:(gpi % 4) * 512 + 512],
                                    t["pring"][:, (gpi % 4) * 512:(gpi % 4) * 512 + 512],
                                    CLIP_HI, masklo[:, rel * 512:(rel + 1) * 512],
                                    ALU.min, ALU.max),
                                    waits=vw, inc="VE")
                            else:
                                ve_stt[j] = p.E("vector", lambda e, gpi=gpi: e.tensor_scalar(
                                    t["pring"][:, (gpi % 4) * 512:(gpi % 4) * 512 + 512],
                                    t["pring"][:, (gpi % 4) * 512:(gpi % 4) * 512 + 512],
                                    CLIP_HI, CLIP_LO, ALU.min, ALU.max),
                                    waits=vw, inc="VE")
                            gpi += 1
                        if j >= 1:
                            jj = j - 1
                            slotb = (g0 + jj) % 4
                            p.E("tensor", lambda e, jj=jj, h=h, pso=pso, slotb=slotb: e.matmul(
                                pso[0:65, :], t["vall"][:, jj * 1040 + h * 65: jj * 1040 + h * 65 + 65],
                                t["pring"][:, slotb * 512: slotb * 512 + 512],
                                start=(jj == 0), stop=(jj == nk - 1)),
                                waits=[("VE", ve_stt[jj])], inc="TE")
                            pv_t[jj] = p.cnt["TE"]
                te_last_score = p.cnt["TE"]
                sv = [("TE", pv_t[nk - 1])]
                if qcg >= 1 and (qcg - 1) in gp_div:
                    sv.append(("GP", gp_div[qcg - 1]))
                ve_stage[qcg] = p.E("vector", lambda e, pso=pso: e.tensor_copy(
                    t["stage"][0:65, :], pso[0:65, :]), waits=sv, inc="VE")
                psb = t["ps_e"] if qcg % 2 == 0 else t["ps_f"]
                bw = [("VE", ve_stage[qcg]), ("GP", g_onesf)]
                if vb_free[0 if qcg % 2 == 0 else 1] is not None:
                    bw.append(vb_free[0 if qcg % 2 == 0 else 1])
                mmb = p.E("tensor", lambda e, psb=psb: e.matmul(
                    psb[0:64, :], t["onesf"][64:65, 0:64], t["stage"][64:65, :],
                    start=True, stop=True), waits=bw, inc="TE")
                sln = p.E("scalar", lambda e, psb=psb: e.activation(
                    t["rb"][0:64, :], psb[0:64, :], AF.Ln),
                    waits=[("TE", mmb)] + ([("GP", gp_div[qcg - 1])] if (qcg - 1) in gp_div else []),
                    inc="SC")
                p.W("scalar", [("SC", sln)])
                srcp = p.E("scalar", lambda e: e.activation(
                    t["rb"][0:64, :], t["rb"][0:64, :], AF.Exp,
                    scale=t["consts"][0:64, 2:3]), inc="SC")
                vb_free[0 if qcg % 2 == 0 else 1] = ("SC", sln)
                gp_div[qcg] = p.E("gpsimd", lambda e, h=h, Q=Q: e.tensor_tensor(
                    t["xn1f"][(h % 2) * 64:(h % 2) * 64 + 64,
                              (h // 2) * S + Q * QB:(h // 2) * S + (Q + 1) * QB],
                    t["stage"][0:64, :], t["rb"][0:64, :], ALU.mult),
                    waits=[("SC", srcp)], inc="GP")
                if dbg == 2 and h == 0 and Q == 0:
                    dump("vall", lambda: t["vall"][:, :], (128, NT * 1040), BF16, [("VE", v_vall), ("GP", g_vones)])
                    dump("qt0", lambda: t["qt"][0:96, :], (96, S), BF16, [("VE", v_build), ("RP", g_rope)])
                    dump("kt0", lambda: t["kt"][0:96, :], (96, S), BF16, [("VE", v_build)])
                    dump("p0", lambda: t["pring"][:, :], (128, 4 * 512), BF16, [("VE", ve_stt[3])])
                    dump("stage0", lambda: t["stage"][0:65, :], (65, 512), F32, [("VE", ve_stage[qcg])])
                    dump("rb0", lambda: t["rb"][0:64, :], (64, 512), F32, [("SC", srcp)])
                qcg += 1
        gp_attn = p.cnt["GP"]

        p.E("sync", lambda e: e.dma_start(
            out=t["wo_sb"][:, :].rearrange("p (n m) -> p n m", m=D),
            in_=t["wob"].rearrange("(n p) m -> p n m", p=128)),
            waits=[("TE", te_last_score), ("VE", p.cnt["VE"]), ("GW", g_uqkv_cast)],
            inc="LD", d=16)
        g_wo = ("LD", p.cnt["LD"])
        ps_free = {0: None, 1: None}
        yt_free = {0: None, 1: None}
        for n in range(NT):
            ypp = n % 2
            yt = t["ytile"][:, ypp * D:(ypp + 1) * D]
            mms = []
            for half in range(2):
                pse = t["ps_e"] if half == 0 else t["ps_f"]
                w = [g_wo, ("GP", gp_attn)]
                if ps_free[half] is not None:
                    w.append(("VE", ps_free[half]))
                for a in range(ND):
                    p.E("tensor", lambda e, pse=pse, a=a, n=n, hf=half: e.matmul(
                        pse[:, :], t["xn1f"][:, a * S + n * 128: a * S + (n + 1) * 128],
                        t["wo_sb"][:, a * D + hf * 512: a * D + hf * 512 + 512],
                        start=(a == 0), stop=(a == ND - 1)),
                        waits=(w if a == 0 else ()), inc=("TE" if a == ND - 1 else None))
                mms.append(p.cnt["TE"])
            ew = [] if yt_free[ypp] is None else [("VE", yt_free[ypp])]
            e0 = p.E("vector", lambda e, yt=yt: e.tensor_copy(
                yt[:, 0:512], t["ps_e"][:, :]), waits=[("TE", mms[0])] + ew, inc="VE")
            e1 = p.E("vector", lambda e, yt=yt: e.tensor_copy(
                yt[:, 512:1024], t["ps_f"][:, :]), waits=[("TE", mms[1])], inc="VE")
            ps_free[0], ps_free[1] = e0, e1
            p.W("vector", [("VE", e1)])
            vsum = p.E("vector", lambda e, yt=yt, n=n: e.reduce_sum(
                st[:, n:n + 1], yt, axis=mybir.AxisListType.X), inc="VE")
            vsq = p.E("vector", lambda e, yt=yt, n=n: e.scalar_tensor_tensor(
                t["sq"][:, :], yt, 1.0, yt, ALU.mult, ALU.mult,
                accum_out=st[:, 16 + n:17 + n]), inc="VE")
            p.W("vector", [("VE", vsq)])
            p.E("vector", lambda e, n=n: e.tensor_scalar_mul(st[:, 32 + n:33 + n], st[:, n:n + 1], 1.0 / D), inc="VE")
            vmsq = p.E("vector", lambda e, n=n: e.tensor_scalar_mul(st[:, 64 + n:65 + n], st[:, 16 + n:17 + n], 1.0 / D), inc="VE")
            smsq = p.E("scalar", lambda e, n=n: e.activation(st[:, 80 + n:81 + n], st[:, 32 + n:33 + n], AF.Square),
                       waits=[("VE", vmsq)], inc="SC")
            vvar = p.E("vector", lambda e, n=n: e.tensor_sub(st[:, 96 + n:97 + n], st[:, 64 + n:65 + n], st[:, 80 + n:81 + n]),
                       waits=[("SC", smsq)], inc="VE")
            slnv = p.E("scalar", lambda e, n=n: e.activation(st[:, 112 + n:113 + n], st[:, 96 + n:97 + n], AF.Ln,
                                                             bias=t["consts"][:, 0:1]),
                       waits=[("VE", vvar)], inc="SC")
            p.W("scalar", [("SC", slnv)])
            sinv = p.E("scalar", lambda e, n=n: e.activation(st[:, 48 + n:49 + n], st[:, 112 + n:113 + n], AF.Exp,
                                                             scale=t["consts"][:, 3:4]), inc="SC")
            p.W("vector", [("SC", sinv)])
            vap = p.E("vector", lambda e, yt=yt, n=n: e.tensor_scalar(
                t["sq"][:, :], yt, st[:, 32 + n:33 + n], st[:, 48 + n:49 + n],
                ALU.subtract, ALU.mult), inc="VE")
            p.W("vector", [("VE", vap)])
            vadd = p.E("vector", lambda e, n=n: e.tensor_add(
                t["xres"][:, n * D:(n + 1) * D], t["xres"][:, n * D:(n + 1) * D],
                t["sq"][:, :]), inc="VE")
            yt_free[ypp] = vap
        v_x1 = p.cnt["VE"]

        if dbg:
            dump("attn", lambda: t["xn1f"][:, :], (128, ND * S), BF16, [("GP", gp_attn)])
        if dbg or dbg == 3:
            pass
        if dbg == 3 or dbg == 1 or dbg == 2:
            dump("x1", lambda: t["xres"][:, :], (128, NT * D), F32, [("VE", v_x1)])
        if dbg == 3:
            dump("mlo", lambda: masklo[:, :], (128, 2048), BF16, [("VE", v_x1)])
            dump("mbs", lambda: t["masks"][:, :], (128, 2048), BF16, [("VE", v_x1)])

    if upto == "C":
        st3 = t["stats"]
        te_c0 = p.cnt["TE"]
        sc_b = p.cnt["SC"]
        ve_b = p.cnt["VE"]

        ld_w3r = p.E("sync", lambda e: e.dma_start(
            out=t["w3srow"][64:65, :], in_=t["w3s_row"][:, :]),
            waits=[("VE", v_x1)], inc="LD", d=16)
        mm3a = p.E("tensor", lambda e: e.matmul(
            t["ps_c"][:, :], t["onesf"][64:65, 0:128], t["w3srow"][64:65, 0:512],
            start=True, stop=True),
            waits=[("LD", ld_w3r), ("GP", g_onesf), ("SC", sc_b), ("VE", ve_b)], inc="TE")
        mm3b = p.E("tensor", lambda e: e.matmul(
            t["ps_d"][:, :], t["onesf"][64:65, 0:128], t["w3srow"][64:65, 512:1024],
            start=True, stop=True), inc="TE")
        v3a = p.E("vector", lambda e: e.tensor_copy(t["w3sb"][:, 0:512], t["ps_c"][:, :]),
                  waits=[("TE", mm3a)], inc="VE")
        v3b = p.E("vector", lambda e: e.tensor_copy(t["w3sb"][:, 512:1024], t["ps_d"][:, :]),
                  waits=[("TE", mm3b)], inc="VE")

        def ln_chain(src_ap, n, waits0=()):
            vsum = p.E("vector", lambda e: e.reduce_sum(
                st3[:, n:n + 1], src_ap, axis=mybir.AxisListType.X), waits=waits0, inc="VE")
            vsq = p.E("vector", lambda e: e.scalar_tensor_tensor(
                t["sqc"][:, :], src_ap, 1.0, src_ap, ALU.mult, ALU.mult,
                accum_out=st3[:, 16 + n:17 + n]), inc="VE")
            p.W("vector", [("VE", vsq)])
            p.E("vector", lambda e: e.tensor_scalar_mul(st3[:, 32 + n:33 + n], st3[:, n:n + 1], 1.0 / D), inc="VE")
            vmsq = p.E("vector", lambda e: e.tensor_scalar_mul(st3[:, 64 + n:65 + n], st3[:, 16 + n:17 + n], 1.0 / D), inc="VE")
            smsq = p.E("scalar", lambda e: e.activation(st3[:, 80 + n:81 + n], st3[:, 32 + n:33 + n], AF.Square),
                       waits=[("VE", vmsq)], inc="SC")
            vvar = p.E("vector", lambda e: e.tensor_sub(st3[:, 96 + n:97 + n], st3[:, 64 + n:65 + n], st3[:, 80 + n:81 + n]),
                       waits=[("SC", smsq)], inc="VE")
            slnv = p.E("scalar", lambda e: e.activation(st3[:, 112 + n:113 + n], st3[:, 96 + n:97 + n], AF.Ln,
                                                        bias=t["consts"][:, 0:1]),
                       waits=[("VE", vvar)], inc="SC")
            p.W("scalar", [("SC", slnv)])
            return p.E("scalar", lambda e: e.activation(st3[:, 48 + n:49 + n], st3[:, 112 + n:113 + n], AF.Exp,
                                                        scale=t["consts"][:, 3:4]), inc="SC")

        mm_w12 = {}
        mm_h = {}
        mm_w3slot = {}
        ps_ef_free = {0: ("VE", ve_b), 1: ("VE", ve_b)}
        ps_q_free = {0: ("SC", sc_b), 1: ("SC", sc_b), 2: ("VE", v3a), 3: ("VE", v3b)}
        first_c_load = True

        ln3_vap = {}

        def emit_ln3(Tn):
            for sub in range(4):
                n = Tn * 4 + sub
                x1t = t["xres"][:, n * D:(n + 1) * D]
                sinv = ln_chain(x1t, n)
                p.W("vector", [("SC", sinv)])
                ln3_vap[(Tn, sub)] = p.E("vector", lambda e, x1t=x1t, n=n, sub=sub: e.tensor_scalar(
                    t["xn2t"][:, sub * D:(sub + 1) * D], x1t,
                    st3[:, 32 + n:33 + n], st3[:, 48 + n:49 + n],
                    ALU.subtract, ALU.mult), inc="VE")

        emit_ln3(0)
        for T in range(NTFF):
            tr_ev = {}
            for sub in range(4):
                vap = ln3_vap[(T, sub)]
                for half in range(2):
                    pst = t["ps_e"] if half == 0 else t["ps_f"]
                    w = [("VE", vap), ps_ef_free[half]]
                    for k in range(4):
                        d = half * 4 + k
                        p.E("tensor", lambda e, pst=pst, d=d, k=k, sub=sub: e.matmul(
                            pst[:, k * 128:(k + 1) * 128],
                            t["xn2t"][:, sub * D + d * 128: sub * D + (d + 1) * 128],
                            t["ident"][:, :], start=True, stop=True),
                            waits=(w if k == 0 else ()), inc=("TE" if k == 3 else None))
                    mmt = p.cnt["TE"]
                    tr_ev[(sub, half)] = p.E("vector", lambda e, pst=pst, sub=sub, half=half: e.tensor_copy(
                        bass.AP(t["xn2f"], half * 4 * TFF + sub * 128, [[ND * TFF, 128], [TFF, 4], [1, 128]]),
                        pst[:, :].rearrange("p (d m) -> p d m", m=128)),
                        waits=[("TE", mmt)], inc="VE")
                    ps_ef_free[half] = ("VE", tr_ev[(sub, half)])
            v_xn2f = p.cnt["VE"]

            if dbg and T == 0:
                dump("xn2f", lambda: t["xn2f"][:, :], (128, ND * TFF), BF16, [("VE", v_xn2f)])

            def xn2f_sl(d):
                return t["xn2f"][:, d * TFF:(d + 1) * TFF]

            ld_w12 = {}

            def emit_w12_load(oc):
                if oc >= NOC:
                    return
                sem = "W12A" if oc % 2 == 0 else "W12B"
                wv = []
                if T == 0 and oc < 2:
                    wv.append(("GF", g_ffncast))
                if first_c[0]:
                    wv.append(("TE", te_c0))
                    first_c[0] = False
                key = (T, oc - 2) if oc >= 2 else (T - 1, NOC - 2 + oc)
                if key in mm_w12:
                    wv.append(("TE", mm_w12[key]))
                p.E("gpsimd", lambda e, oc=oc: e.dma_start(
                    out=bass.AP(t["w12buf"], (oc % 2) * 1024, [[4096, 128], [128, 8], [1, 128]]),
                    in_=t["w1b"][oc].rearrange("(ic p) o -> p ic o", p=128)),
                    waits=wv, inc=sem, d=16)
                ld_w12[oc] = p.E("gpsimd", lambda e, oc=oc: e.dma_start(
                    out=bass.AP(t["w12buf"], 2048 + (oc % 2) * 1024, [[4096, 128], [128, 8], [1, 128]]),
                    in_=t["w2b"][oc].rearrange("(ic p) o -> p ic o", p=128)),
                    inc=sem, d=16)

            first_c = [first_c_load]
            first_c_load = False
            emit_w12_load(0)
            emit_w12_load(1)

            v_a = None
            for oc in range(NOC):
                sem = "W12A" if oc % 2 == 0 else "W12B"
                pse = t["ps_e"] if oc % 2 == 0 else t["ps_f"]
                w = [(sem, ld_w12[oc]), ("VE", v_xn2f), ps_ef_free[oc % 2]]
                for ic in range(ND):
                    p.E("tensor", lambda e, pse=pse, ic=ic, oc=oc: e.matmul(
                        pse[:, :], t["w12buf"][:, (oc % 2) * 1024 + ic * 128:(oc % 2) * 1024 + (ic + 1) * 128],
                        xn2f_sl(ic), start=(ic == 0), stop=(ic == ND - 1)),
                        waits=(w if ic == 0 else ()), inc=("TE" if ic == ND - 1 else None))
                m1 = p.cnt["TE"]
                sw = [("TE", m1)]
                if dbg and T == 0 and oc == 0:
                    dva = p.E("vector", lambda e, pse=pse: e.tensor_copy(
                        t["sqc"][:, 0:512], pse[:, :]), waits=[("TE", m1)], inc="VE")
                    dump("t1raw", lambda: t["sqc"][:, 0:512], (128, 512), F32, [("VE", dva)])
                    sw.append(("VE", dva))
                if v_a is not None:
                    sw.append(("VE", v_a))
                sgate = p.E("scalar", lambda e, pse=pse, oc=oc: e.activation(
                    t["b_sb"][:, 0:512], pse[:, :], AF.Silu, scale=t["w1s_sb"][:, oc:oc + 1]),
                    waits=sw, inc="SC")
                for ic in range(ND):
                    p.E("tensor", lambda e, pse=pse, ic=ic, oc=oc: e.matmul(
                        pse[:, :], t["w12buf"][:, 2048 + (oc % 2) * 1024 + ic * 128: 2048 + (oc % 2) * 1024 + (ic + 1) * 128],
                        xn2f_sl(ic), start=(ic == 0), stop=(ic == ND - 1)),
                        waits=([("SC", sgate)] if ic == 0 else ()), inc=("TE" if ic == ND - 1 else None))
                m2 = p.cnt["TE"]
                mm_w12[(T, oc)] = m2
                emit_w12_load(oc + 2)
                vup = p.E("vector", lambda e, pse=pse, oc=oc: e.tensor_scalar_mul(
                    t["b_sb"][:, 512:1024], pse[:, :], t["w2s_sb"][:, oc:oc + 1]),
                    waits=[("TE", m2)], inc="VE")
                ps_ef_free[oc % 2] = ("VE", vup)
                p.W("vector", [("VE", vup)])
                v_a = p.E("vector", lambda e, oc=oc: e.tensor_mul(
                    t["a_sb"][:, oc * TFF:(oc + 1) * TFF], t["b_sb"][:, 0:512],
                    t["b_sb"][:, 512:1024]),
                    waits=[("SC", sgate)], inc="VE")

            if dbg and T == 0:
                dump("asb", lambda: t["a_sb"][:, 0:4096], (128, 4096), BF16, [("VE", v_a)])

            src_ready = ("VE", v_a)
            for stage, (hb, hs, srcb, dstb) in enumerate((
                    ("h1b", "h1s_sb", "a_sb", "b_sb"),
                    ("h2b", "h2s_sb", "b_sb", "a_sb"))):
                ld_h = {}

                def emit_h_load(oc, hb=hb, stage=stage, ld_h_ref=None):
                    if oc >= NOC:
                        return
                    sem = "HBA" if oc % 2 == 0 else "HBB"
                    wv = []
                    key = (T, stage, oc - 2) if oc >= 2 else (
                        (T, stage - 1, NOC - 2 + oc) if stage > 0 else
                        ((T - 1, 1, NOC - 2 + oc) if T > 0 else None))
                    if key in mm_h:
                        wv.append(("TE", mm_h[key]))
                    elif T == 0 and stage == 0:
                        wv.append(("TE", te_c0))
                        wv.append(("GF", g_ffncast))
                    ld_h[oc] = p.E("gpsimd", lambda e, oc=oc, hb=hb: e.dma_start(
                        out=bass.AP(t["hbuf"], (oc % 2) * 4096, [[8192, 128], [128, 32], [1, 128]]),
                        in_=t[hb][oc].rearrange("(ic p) o -> p ic o", p=128)),
                        waits=wv, inc=sem, d=16)

                emit_h_load(0)
                emit_h_load(1)
                for oc in range(NOC):
                    sem = "HBA" if oc % 2 == 0 else "HBB"
                    pse = t["ps_e"] if oc % 2 == 0 else t["ps_f"]
                    w = [(sem, ld_h[oc]), ps_ef_free[oc % 2]]
                    if oc <= 1:
                        w.append(src_ready)
                    for ic in range(NOC):
                        p.E("tensor", lambda e, pse=pse, ic=ic, oc=oc, srcb=srcb: e.matmul(
                            pse[:, :], t["hbuf"][:, (oc % 2) * 4096 + ic * 128:(oc % 2) * 4096 + (ic + 1) * 128],
                            t[srcb][:, ic * TFF:(ic + 1) * TFF],
                            start=(ic == 0), stop=(ic == NOC - 1)),
                            waits=(w if ic == 0 else ()), inc=("TE" if ic == NOC - 1 else None))
                    mm_h[(T, stage, oc)] = p.cnt["TE"]
                    emit_h_load(oc + 2)
                    sg = p.E("scalar", lambda e, pse=pse, oc=oc, hs=hs, dstb=dstb: e.activation(
                        t[dstb][:, oc * TFF:(oc + 1) * TFF], pse[:, :], AF.Gelu,
                        scale=t[hs][:, oc:oc + 1]),
                        waits=[("TE", mm_h[(T, stage, oc)])], inc="SC")
                    ps_ef_free[oc % 2] = ("SC", sg)
                src_ready = ("SC", p.cnt["SC"])
                if dbg and T == 0 and stage == 0:
                    dump("bsb", lambda: t["b_sb"][:, 0:4096], (128, 4096), BF16, [src_ready])
                if dbg and T == 0 and stage == 1:
                    dump("csb", lambda: t["a_sb"][:, 0:4096], (128, 4096), BF16, [src_ready])
            c_done = src_ready
            if T + 1 < NTFF:
                emit_ln3(T + 1)

            psq = [(t["ps_s0"], 0), (t["ps_s0"], 512), (t["ps_s1"], 0), (t["ps_s1"], 512)]
            for pas in range(2):
                ld_w3 = {}

                def emit_w3_load(ic, pas=pas, ld_w3_ref=None):
                    if ic >= NOC:
                        return
                    slot = ic % 6
                    sem = "W3%d" % slot
                    prev_in_prior = slot + 30 if slot < 2 else slot + 24
                    wv = []
                    if (T, pas, ic - 6) in mm_w3slot:
                        wv.append(("TE", mm_w3slot[(T, pas, ic - 6)]))
                    elif pas > 0 and (T, pas - 1, prev_in_prior) in mm_w3slot:
                        wv.append(("TE", mm_w3slot[(T, pas - 1, prev_in_prior)]))
                    elif T > 0 and (T - 1, 1, prev_in_prior) in mm_w3slot:
                        wv.append(("TE", mm_w3slot[(T - 1, 1, prev_in_prior)]))
                    if not wv and T == 0 and pas == 0:
                        wv.append(("TE", te_c0))
                        wv.append(("GF", g_ffncast))
                    ld_w3[ic] = (sem, p.E("gpsimd", lambda e, ic=ic, slot=slot: e.dma_start(
                        out=t["w3buf"][:, slot * D:(slot + 1) * D], in_=t["w3b"][ic]),
                        waits=wv, inc=sem, d=16))

                for ic in range(6):
                    emit_w3_load(ic)
                for ic in range(NOC):
                    slot = ic % 6
                    for qi in range(4):
                        subL, oh = 2 * pas + qi // 2, qi % 2
                        ps = psq[qi]
                        w = []
                        if qi == 0:
                            w.append(ld_w3[ic])
                        if ic == 0:
                            w.append(c_done)
                            w.append(ps_q_free[qi])
                        p.E("tensor", lambda e, ps=ps, ic=ic, subL=subL, oh=oh, slot=slot: e.matmul(
                            ps[0][:, ps[1]:ps[1] + 512],
                            t["a_sb"][:, ic * TFF + subL * 128: ic * TFF + subL * 128 + 128],
                            t["w3buf"][:, slot * D + oh * 512: slot * D + oh * 512 + 512],
                            start=(ic == 0), stop=(ic == NOC - 1)),
                            waits=[x for x in w if x], inc=("TE" if qi == 3 else None))
                    mm_w3slot[(T, pas, ic)] = p.cnt["TE"]
                    emit_w3_load(ic + 6)
                mm_last = p.cnt["TE"]
                for qi2 in range(2):
                    subL = 2 * pas + qi2
                    n = T * 4 + subL
                    ev0 = p.E("vector", lambda e, qi2=qi2: e.tensor_tensor(
                        t["ztile"][:, 0:512], psq[qi2 * 2][0][:, psq[qi2 * 2][1]:psq[qi2 * 2][1] + 512],
                        t["w3sb"][:, 0:512], ALU.mult),
                        waits=[("TE", mm_last)], inc="VE")
                    ev1 = p.E("vector", lambda e, qi2=qi2: e.tensor_tensor(
                        t["ztile"][:, 512:1024], psq[qi2 * 2 + 1][0][:, psq[qi2 * 2 + 1][1]:psq[qi2 * 2 + 1][1] + 512],
                        t["w3sb"][:, 512:1024], ALU.mult),
                        inc="VE")
                    ps_q_free[qi2 * 2] = ("VE", ev0)
                    ps_q_free[qi2 * 2 + 1] = ("VE", ev1)
                    p.W("vector", [("VE", ev1)])
                    if dbg and T == 0 and pas == 0 and qi2 == 0:
                        dump("zt0", lambda: t["ztile"][:, :], (128, D), F32, [("VE", ev1)])
                    if dbg and T == 1 and pas == 0 and qi2 == 0:
                        dva2 = p.E("vector", lambda e: e.tensor_copy(
                            t["outt"][:, :], t["ztile"][:, :]), waits=[("VE", ev1)], inc="VE")
                        dump("zt4", lambda: t["outt"][:, :], (128, D), F32, [("VE", dva2)])
                        dump("xn2fT1", lambda: t["xn2f"][:, :], (128, ND * TFF), BF16, [("VE", v_xn2f)])
                        dump("csbT1", lambda: t["a_sb"][:, 0:4096], (128, 4096), BF16, [c_done])
                    sinv = ln_chain(t["ztile"][:, :], n)
                    p.W("vector", [("SC", sinv)])
                    vap = p.E("vector", lambda e, n=n: e.tensor_scalar(
                        t["sqc"][:, :], t["ztile"][:, :],
                        st3[:, 32 + n:33 + n], st3[:, 48 + n:49 + n],
                        ALU.subtract, ALU.mult), inc="VE")
                    p.W("vector", [("VE", vap)])
                    vadd = p.E("vector", lambda e, n=n: e.tensor_add(
                        t["xres"][:, n * D:(n + 1) * D], t["xres"][:, n * D:(n + 1) * D],
                        t["sqc"][:, :]), inc="VE")
                    p.E("gpsimd", lambda e, n=n: e.dma_start(
                        out=t["y"][n * 128:(n + 1) * 128, :],
                        in_=t["xres"][:, n * D:(n + 1) * D]),
                        waits=[("VE", vadd)], inc="ST", d=16)

    p.W("sync", [("ST", p.cnt["ST"])])
    p.W("gpsimd", [("GW", p.cnt["GW"]), ("GF", p.cnt["GF"])])

    p.replay()
    for g in reversed(pg):
        g.__exit__(None, None, None)
    return nc, dbg_outs


def _host_inputs(inp, core):
    f = np.ascontiguousarray
    W_QR = np.asarray(inp["L0_attn_W_QR"])
    W_KR = np.asarray(inp["L0_attn_W_KR"])
    d = {
        "x": f(np.asarray(inp["L0_attn_x"])[core]),
        "wdq": f(np.asarray(inp["L0_attn_W_DQ"])),
        "wdkv": f(np.asarray(inp["L0_attn_W_DKV"])),
        "wuq": f(np.asarray(inp["L0_attn_W_UQ"])),
        "wuk": f(np.asarray(inp["L0_attn_W_UK"])),
        "wuv": f(np.asarray(inp["L0_attn_W_UV"])),
        "wqr": f(np.concatenate([W_QR[:, 0::2], W_QR[:, 1::2]], axis=1)),
        "wkr": f(np.concatenate([W_KR[:, 0::2], W_KR[:, 1::2]], axis=1)),
        "wo": f(np.asarray(inp["L0_attn_W_O"])),
        "w1t": f(np.asarray(inp["L0_ffn_w1_w"]).T.reshape(D, NOC, 128).transpose(1, 0, 2)),
        "w2t": f(np.asarray(inp["L0_ffn_w2_w"]).T.reshape(D, NOC, 128).transpose(1, 0, 2)),
        "h1t": f(np.asarray(inp["L0_ffn_h1_w"]).T.reshape(DHID, NOC, 128).transpose(1, 0, 2)),
        "h2t": f(np.asarray(inp["L0_ffn_h2_w"]).T.reshape(DHID, NOC, 128).transpose(1, 0, 2)),
        "w3t": f(np.asarray(inp["L0_ffn_w3_w"]).T.reshape(NOC, 128, D)),
        "w1s": f(np.asarray(inp["L0_ffn_w1_s"]).reshape(NOC, 128).T),
        "w2s": f(np.asarray(inp["L0_ffn_w2_s"]).reshape(NOC, 128).T),
        "h1s": f(np.asarray(inp["L0_ffn_h1_s"]).reshape(NOC, 128).T),
        "h2s": f(np.asarray(inp["L0_ffn_h2_s"]).reshape(NOC, 128).T),
        "w3s": f(np.asarray(inp["L0_ffn_w3_s"]).reshape(ND, 128).T),
        "w3s_row": f(np.asarray(inp["L0_ffn_w3_s"]).reshape(1, D)),
    }
    fidx = np.arange(128) % 16
    inv = (10000.0 ** (-(2.0 * fidx) / DHR)).astype(np.float64)
    ang = np.arange(S, dtype=np.float64)[None, :] * inv[:, None]
    d["costab"] = np.cos(ang).astype(np.float32)
    d["sintab"] = np.sin(ang).astype(np.float32)
    return d


def kernel(**inputs):
    nc, _ = build_program(dbg=0)
    in_maps = [_host_inputs(inputs, c) for c in range(8)]
    res = run_bass_kernel_spmd(nc, in_maps, core_ids=list(range(8)))
    return np.stack([np.asarray(res.results[c]["y"]) for c in range(8)], axis=0)


# version 09
# speedup vs baseline: 1.0540x; 1.0540x over previous
import math
import numpy as np
import concourse.bass as bass
import concourse.mybir as mybir
from concourse.bass_utils import run_bass_kernel_spmd

F32 = mybir.dt.float32
BF16 = mybir.dt.bfloat16
AF = mybir.ActivationFunctionType
ALU = mybir.AluOpType

S = 2048
D = 1024
H = 16
DH = 64
DHR = 32
DQK = DH + DHR
DL = 256
NT = 16
ND = 8
NL = 2
QB = 512
NQ = 4
KTL = 128
DHID = 4096
NOC = 32
TFF = 512
NTFF = 4
SM_SCALE = 1.0 / math.sqrt(float(DQK))
CLIP_HI = math.exp(80.0 * SM_SCALE)
CLIP_LO = math.exp(-80.0 * SM_SCALE)
EPS = 1e-5

ENGINES = ("sync", "gpsimd", "vector", "scalar", "tensor")
EXPSPLIT = 1
PAIRED = 1
BARRIER = 0


class Plan:

    def __init__(self, nc):
        self.nc = nc
        self.ops = {e: [] for e in ENGINES}
        self.sems = {}
        self.cnt = {}
        self._es = []

    def sem(self, name):
        g = self.nc.semaphore(name)
        h = g.__enter__()
        self._es.append(g)
        self.sems[name] = h
        self.cnt[name] = 0

    def E(self, eng, fn, waits=(), inc=None, d=1):
        w = tuple(waits)
        tgt = None
        if inc is not None:
            self.cnt[inc] += d
            tgt = self.cnt[inc]
        sems = self.sems

        def g(e, fn=fn, w=w, inc=inc, d=d):
            for (s, v) in w:
                e.wait_ge(sems[s], v)
            ins = fn(e)
            if inc is not None:
                ins.then_inc(sems[inc], d)
        self.ops[eng].append(g)
        return tgt

    def W(self, eng, waits):
        w = tuple(waits)
        sems = self.sems

        def g(e, w=w):
            for (s, v) in w:
                e.wait_ge(sems[s], v)
        self.ops[eng].append(g)

    def replay(self):
        nc = self.nc
        with nc.Block() as block:
            @block.sync
            def _(e):
                for f in self.ops["sync"]:
                    f(e)

            @block.gpsimd
            def _(e):
                for f in self.ops["gpsimd"]:
                    f(e)

            @block.vector
            def _(e):
                for f in self.ops["vector"]:
                    f(e)

            @block.scalar
            def _(e):
                for f in self.ops["scalar"]:
                    f(e)

            @block.tensor
            def _(e):
                for f in self.ops["tensor"]:
                    f(e)
        for g in reversed(self._es):
            g.__exit__(None, None, None)


SB = {}


def _sb(name, off, nbytes, life):
    SB[name] = (off, nbytes, life)
    return off


_sb("xres",   0,      65536, "P")
_sb("stats",  65536,  512,   "P")
_sb("ident",  66048,  256,   "P")
_sb("masks",  66304,  4096,  "P")
_sb("scales", 70400,  768,   "P")
_sb("cos",    71168,  8192,  "A")
_sb("sin",    79360,  8192,  "A")
_sb("x1q",    87552,  8192,  "A")
_sb("x2q",    95744,  8192,  "A")
_sb("ones",   87552,  1024,  "A0")
_sb("wdq",    103936, 4096,  "A")
_sb("wdkv",   108032, 4096,  "A")
_sb("wqr",    112128, 2048,  "A")
_sb("wkr",    114176, 512,   "A")
_sb("xn1t",   114688, 4096,  "A")
_sb("tmpa",   118784, 4096,  "A")
_sb("krx",    122880, 8192,  "A")
_sb("sq",     131072, 4096,  "AB")
_sb("xn1f",   135168, 32768, "AB")
_sb("cq",     167936, 8192,  "AB")
_sb("ckv",    176128, 8192,  "AB")
_sb("r12",    184320, 16384, "AB")
_sb("krr",    200704, 4096,  "AB")
_sb("vall",   71168,  33280, "B")
_sb("wuq",    104448, 4096,  "B")
_sb("wuk",    108544, 4096,  "B")
_sb("wuv",    112640, 4096,  "B")
_sb("qt",     116736, 4096,  "B")
_sb("kt",     120832, 4096,  "B")
_sb("pring",  124928, 4096,  "B")
_sb("stage",  129024, 2048,  "B")
_sb("rb",     204800, 2048,  "B")
_sb("onesf",  206848, 512,   "P")
_sb("wo",     104448, 16384, "BW")
_sb("ytile",  120832, 8192,  "BW")
_sb("xn2t",   71168,  8192,  "C")
_sb("xn2f",   79360,  8192,  "C")
_sb("a_sb",   87552,  32768, "C")
_sb("b_sb",   120320, 32768, "C")
_sb("w12buf", 153088, 8192,  "C")
_sb("hbuf",   161280, 16384, "C")
_sb("w3buf",  177664, 12288, "C")
_sb("w3sb",   189952, 4096,  "C")
_sb("ztile",  194048, 4096,  "C")
_sb("outt",   198144, 4096,  "C")
_sb("sqc",    202240, 4096,  "C")


def _check_map():
    lim = 207 * 1024
    items = [(n, o, b, l) for n, (o, b, l) in SB.items() if b > 0]
    for n, o, b, l in items:
        assert o + b <= lim + 1024, (n, o + b)
    phases = {"A": ("P", "A", "AB", "ABC"),
              "B": ("P", "AB", "ABC", "B"),
              "BW": ("P", "ABC", "BW", "AB"),
              "C": ("P", "ABC", "C")}
    for ph, lives in phases.items():
        acc = []
        for n, o, b, l in items:
            if l in lives:
                acc.append((o, o + b, n))
        acc.sort()
        for i in range(len(acc) - 1):
            assert acc[i][1] <= acc[i + 1][0], (ph, acc[i], acc[i + 1])


_check_map()


def build_program(dbg=0, upto="C", no_ffncast=0):
    nc = bass.Bass("TRN2", target_bir_lowering=False)
    p = Plan(nc)
    t = {}

    def din(name, shape, dtype=F32):
        t[name] = nc.dram_tensor(name, list(shape), dtype, kind="ExternalInput").ap()
    din("x", (S, D))
    din("wdq", (D, DL)); din("wdkv", (D, DL))
    din("wuq", (DL, D)); din("wuk", (DL, D)); din("wuv", (DL, D))
    din("wqr", (DL, H * DHR))
    din("wkr", (D, DHR))
    din("wo", (D, D))
    din("costab", (128, S)); din("sintab", (128, S))
    din("w1t", (NOC, D, 128)); din("w2t", (NOC, D, 128))
    din("h1t", (NOC, DHID, 128)); din("h2t", (NOC, DHID, 128))
    din("w3t", (NOC, 128, D))
    din("w1s", (128, NOC)); din("w2s", (128, NOC)); din("h1s", (128, NOC)); din("h2s", (128, NOC))
    din("w3s", (128, ND))
    din("w3s_row", (1, D))
    t["y"] = nc.dram_tensor("y", [S, D], F32, kind="ExternalOutput").ap()
    for nm, shp in (("uqb", (DL, D)), ("ukb", (DL, D)),
                    ("uvb", (DL, D)), ("wob", (D, D))):
        t[nm] = nc.dram_tensor(nm, list(shp), BF16, kind="Internal").ap()
    for nm, shp in (("w1b", (NOC, D, 128)), ("w2b", (NOC, D, 128)),
                    ("h1b", (NOC, DHID, 128)), ("h2b", (NOC, DHID, 128)),
                    ("w3b", (NOC, 128, D))):
        t[nm] = nc.dram_tensor(nm, list(shp), BF16, kind="Internal").ap()

    MAP_BYTES = 206336
    arena = nc.alloc_sbuf_tensor("arena", [128, MAP_BYTES], mybir.dt.uint8)
    SB_BASE = nc.lookup_mloc(arena).addr

    def sb(name, shape, dtype, key=None, extra=0):
        off, nbytes, _ = SB[key or name]
        h = nc.alloc_sbuf_tensor_at(name, list(shape), dtype, offset=SB_BASE + off + extra)
        t[name] = h
        return h
    sb("xres", (128, NT * D), F32)
    sb("stats", (128, 128), F32)
    sb("ident", (128, 128), BF16)
    sb("masks", (128, 4 * 512), BF16)
    sb("w1s_sb", (128, 32), F32, key="scales", extra=0)
    sb("w2s_sb", (128, 32), F32, key="scales", extra=128)
    sb("h1s_sb", (128, 32), F32, key="scales", extra=256)
    sb("h2s_sb", (128, 32), F32, key="scales", extra=384)
    sb("w3s_sb", (128, 8), F32, key="scales", extra=512)
    sb("consts", (128, 8), F32, key="scales", extra=576)
    sb("cos", (128, S), F32); sb("sin", (128, S), F32)
    sb("x1q", (128, 2 * S), BF16); sb("x2q", (128, 2 * S), BF16)
    sb("ones", (128, 512), BF16)
    sb("wdq_sb", (128, ND * DL), BF16, key="wdq")
    sb("wdkv_sb", (128, ND * DL), BF16, key="wdkv")
    sb("wqr_sb", (128, NL * 512), BF16, key="wqr")
    sb("wkr_sb", (128, ND * DHR), BF16, key="wkr")
    sb("xn1t", (128, 2 * D), BF16)
    sb("tmpa", (128, 2 * 512), F32)
    sb("krx", (16, 2 * S), BF16)
    sb("sq", (128, D), F32)
    sb("xn1f", (128, ND * S), BF16)
    sb("cq", (128, NL * S), BF16)
    sb("ckv", (128, NL * S), BF16)
    sb("r12", (128, 2 * 2 * S), BF16)
    sb("krr", (96, S), BF16)
    sb("vall", (128, NT * 1040), BF16)
    sb("wuq_sb", (128, NL * D), BF16, key="wuq")
    sb("wuk_sb", (128, NL * D), BF16, key="wuk")
    sb("wuv_sb", (128, NL * D), BF16, key="wuv")
    sb("qt", (128, S), BF16)
    sb("kt", (128, S), BF16)
    sb("pring", (128, 4 * 512), BF16)
    sb("stage", (128, 512), F32)
    sb("wo_sb", (128, ND * D), BF16, key="wo")
    sb("ytile", (128, 2 * D), F32)
    sb("rb", (64, 512), F32)
    sb("onesf", (65, 128), F32)
    sb("xn2t", (128, 4 * D), BF16)
    sb("xn2f", (128, ND * TFF), BF16)
    sb("a_sb", (128, NOC * TFF), BF16)
    sb("b_sb", (128, NOC * TFF), BF16)
    sb("w12buf", (128, 4 * 8 * 128), BF16)
    sb("hbuf", (128, 2 * NOC * 128), BF16)
    sb("w3buf", (128, 8 * D), BF16)
    sb("w3sb", (128, D), F32)
    sb("ztile", (128, D), F32)
    sb("outt", (128, D), F32)
    sb("sqc", (128, D), F32)
    t["w3srow"] = nc.alloc_sbuf_tensor_at("w3srow", [65, D], F32, offset=SB_BASE + SB["ztile"][0])

    pg = []
    def ps(name, shape, dtype):
        g = nc.psum_tensor(name, list(shape), dtype)
        h = g.__enter__(); pg.append(g); t[name] = h
        return h
    ps("ps_s0", (128, 1024), F32); ps("ps_s1", (128, 1024), F32)
    ps("ps_c", (128, 512), F32); ps("ps_d", (128, 512), F32)
    ps("ps_e", (128, 512), F32); ps("ps_f", (128, 512), F32)

    for s in ("LDX", "LD", "RP", "GW", "GF", "SD", "TE", "VE", "SC", "GP", "ST", "TEA",
              "W12A", "W12B", "HBA", "HBB", "W30", "W31", "W32", "W33", "W34", "W35"):
        p.sem(s)

    dbg_outs = []
    def dump(name, src_fn, shape, dtype, waits):
        if not dbg or (dbg == 3 and name not in ("x1", "mlo", "mbs", "attn")):
            return
        ap = nc.dram_tensor("dbg_" + name, list(shape), dtype, kind="ExternalOutput").ap()
        p.E("sync", lambda e, ap=ap, s=src_fn: e.dma_start(out=ap, in_=s()),
            waits=waits, inc="ST", d=16)
        dbg_outs.append(name)

    ld_x = p.E("sync", lambda e: e.dma_start(
        out=t["xres"][:, :].rearrange("p (n m) -> p n m", m=D),
        in_=t["x"].rearrange("(n p) m -> p n m", p=128)), inc="LDX", d=16)
    ld_cos = p.E("sync", lambda e: e.dma_start(out=t["cos"][:, :], in_=t["costab"][:, :]), inc="LD", d=16)
    ld_sin = p.E("sync", lambda e: e.dma_start(out=t["sin"][:, :], in_=t["sintab"][:, :]), inc="LD", d=16)
    for nm, dst in (("w1s", "w1s_sb"), ("w2s", "w2s_sb"), ("h1s", "h1s_sb"), ("h2s", "h2s_sb"),
                    ("w3s", "w3s_sb")):
        p.E("sync", lambda e, nm=nm, dst=dst: e.dma_start(
            out=t[dst][:, :], in_=t[nm][:, :]), inc="LD", d=16)
    ld_misc = p.cnt["LD"]

    p.E("gpsimd", lambda e: e.memset(t["consts"][:, 0:1], EPS), inc="GP")
    p.E("gpsimd", lambda e: e.memset(t["consts"][:, 1:2], SM_SCALE), inc="GP")
    p.E("gpsimd", lambda e: e.memset(t["consts"][:, 2:3], -1.0), inc="GP")
    p.E("gpsimd", lambda e: e.memset(t["consts"][:, 3:4], -0.5), inc="GP")
    g_consts = p.cnt["GP"]
    p.E("gpsimd", lambda e: e.memset(t["onesf"][64:65, :], 1.0), inc="GP")
    g_onesf = p.cnt["GP"]
    g1 = p.E("gpsimd", lambda e: e.memset(t["ones"][:, :], 1.0), inc="GP")
    p.W("gpsimd", [("GP", g1)])
    g_id = p.E("gpsimd", lambda e: e.affine_select(
        t["ident"][:, :], t["ones"][:, 0:128], [[1, 128]], ALU.is_equal, 0.0,
        base=0, channel_multiplier=-1), inc="GP")
    gz = p.E("gpsimd", lambda e: e.memset(t["ones"][:, :], 0.0), inc="GP")
    p.W("gpsimd", [("GP", gz)])
    for j in range(4):
        p.E("gpsimd", lambda e, j=j: e.affine_select(
            t["masks"][:, j * 512:(j + 1) * 512], t["ones"][:, :], [[1, 512]],
            ALU.is_ge, -1.0e5, base=-j * 128, channel_multiplier=-1), inc="GP")
    g_masks = p.cnt["GP"]

    def cast_load(dst, src, rearr, waits=()):
        return p.E("gpsimd", lambda e: e.dma_start(
            out=t[dst][:, :].rearrange("p (n m) -> p n m", m=rearr),
            in_=t[src].rearrange("(n p) m -> p n m", p=128)), waits=waits, inc="GW", d=16)
    cast_load("wdq_sb", "wdq", DL)
    cast_load("wdkv_sb", "wdkv", DL)
    cast_load("wqr_sb", "wqr", 512)
    cast_load("wkr_sb", "wkr", DHR)
    g_wdq = g_wdkv = g_wqr = g_wkr = p.cnt["GW"]
    p.W("gpsimd", [("GW", p.cnt["GW"])])
    for a, b in (("wuq", "uqb"), ("wuk", "ukb"), ("wuv", "uvb"), ("wo", "wob")):
        p.E("gpsimd", lambda e, a=a, b=b: e.dma_start(
            out=t[b].rearrange("a b -> (a b)"), in_=t[a].rearrange("a b -> (a b)")),
            inc="GW", d=16)
    g_uqkv_cast = p.cnt["GW"]
    flood_ld_gate = p.cnt["LD"] + 48
    for src, dst in (("w1t", "w1b"), ("w2t", "w2b"), ("w3t", "w3b")):
        for oc in range(NOC):
            p.E("gpsimd", lambda e, src=src, dst=dst, oc=oc: e.dma_start(
                out=t[dst][oc].rearrange("a b -> (a b)"),
                in_=t[src][oc].rearrange("a b -> (a b)")),
                inc="GF", d=16)

    st = t["stats"]
    sc_sq = []
    ve_sum = []
    for n in range(NT):
        xt = t["xres"][:, n * D:(n + 1) * D]
        sc_sq.append(p.E("vector", lambda e, xt=xt, n=n: e.scalar_tensor_tensor(
            t["sq"][:, :], xt, 1.0, xt, ALU.mult, ALU.mult,
            accum_out=st[:, 16 + n:17 + n]),
            waits=[("LDX", ld_x)] if n == 0 else (), inc="VE"))
        ve_sum.append(p.E("vector", lambda e, xt=xt, n=n: e.reduce_sum(
            st[:, n:n + 1], xt, axis=mybir.AxisListType.X), inc="VE"))
    p.W("vector", [("VE", ve_sum[-1])])
    p.E("vector", lambda e: e.tensor_scalar_mul(st[:, 32:48], st[:, 0:16], 1.0 / D), inc="VE")
    v_msq = p.E("vector", lambda e: e.tensor_scalar_mul(st[:, 64:80], st[:, 16:32], 1.0 / D), inc="VE")
    s_msq = p.E("scalar", lambda e: e.activation(st[:, 80:96], st[:, 32:48], AF.Square),
                waits=[("VE", v_msq)], inc="SC")
    v_var = p.E("vector", lambda e: e.tensor_sub(st[:, 96:112], st[:, 64:80], st[:, 80:96]),
                waits=[("SC", s_msq)], inc="VE")
    s_lnv = p.E("scalar", lambda e: e.activation(st[:, 112:128], st[:, 96:112], AF.Ln,
                                                 bias=t["consts"][:, 0:1]),
                waits=[("VE", v_var), ("GP", g_consts)], inc="SC")
    p.W("scalar", [("SC", s_lnv)])
    s_inv = p.E("scalar", lambda e: e.activation(st[:, 48:64], st[:, 112:128], AF.Exp,
                                                 scale=t["consts"][:, 3:4]), inc="SC")
    v_inv = s_inv

    ev_done = [None, None]
    for n in range(NT):
        pp = n % 2
        xt = t["xres"][:, n * D:(n + 1) * D]
        xn = t["xn1t"][:, pp * D:(pp + 1) * D]
        w = [("SC", v_inv)]
        v_app = p.E("vector", lambda e, xt=xt, xn=xn, n=n: e.tensor_scalar(
            xn, xt, st[:, 32 + n:33 + n], st[:, 48 + n:49 + n], ALU.subtract, ALU.mult),
            waits=w, inc="VE")
        for half in range(2):
            pst = t["ps_e"] if half == 0 else t["ps_f"]
            w = [("VE", v_app), ("GP", g_id)]
            if ev_done[half] is not None:
                w.append(("VE", ev_done[half]))
            for k in range(4):
                d = half * 4 + k
                p.E("tensor", lambda e, pst=pst, xn=xn, d=d, k=k: e.matmul(
                    pst[:, k * 128:(k + 1) * 128], xn[:, d * 128:(d + 1) * 128],
                    t["ident"][:, :], start=True, stop=True),
                    waits=(w if k == 0 else ()), inc=("TE" if k == 3 else None))
            trd = p.cnt["TE"]
            ev_done[half] = p.E("vector", lambda e, pst=pst, n=n, half=half: e.tensor_copy(
                bass.AP(t["xn1f"], half * 4 * S + n * 128, [[NT * D, 128], [S, 4], [1, 128]]),
                pst[:, :].rearrange("p (d m) -> p d m", m=128)),
                waits=[("TE", trd)], inc="VE")
    v_xn1f = p.cnt["VE"]

    def xn1f_sl(d, q):
        return t["xn1f"][:, d * S + q * QB: d * S + q * QB + QB]

    pb_free = {0: ev_done[0], 1: ev_done[1]}
    pp = 0

    def proj(dst, dstoff, wsb, wslice_fn, mbase, msize, q, extra_w=()):
        nonlocal pp
        pse = t["ps_e"] if pp == 0 else t["ps_f"]
        w = [("VE", v_xn1f)] + list(extra_w)
        if pb_free[pp] is not None:
            w.append(("VE", pb_free[pp]))
        for d in range(ND):
            p.E("tensor", lambda e, pse=pse, d=d, q=q, wf=wslice_fn: e.matmul(
                pse[mbase:mbase + msize, :], wf(d), xn1f_sl(d, q),
                start=(d == 0), stop=(d == ND - 1)),
                waits=(w if d == 0 else ()), inc=("TE" if d == ND - 1 else None))
        mm = p.cnt["TE"]
        ev = p.E("vector", lambda e, pse=pse, dst=dst, o=dstoff: e.tensor_copy(
            t[dst][mbase:mbase + msize, o:o + QB], pse[mbase:mbase + msize, :]),
            waits=[("TE", mm)], inc="VE")
        pb_free[pp] = ev
        pp ^= 1
        return ev

    cq_done = []
    for l in range(NL):
        for q in range(NQ):
            cq_done.append(proj("cq", l * S + q * QB,
                                "wdq_sb", lambda d, l=l: t["wdq_sb"][:, d * DL + l * 128: d * DL + (l + 1) * 128],
                                0, 128, q, extra_w=[("GW", g_wdq)]))
    ckv_done = []
    for l in range(NL):
        for q in range(NQ):
            ckv_done.append(proj("ckv", l * S + q * QB,
                                 "wdkv_sb", lambda d, l=l: t["wdkv_sb"][:, d * DL + l * 128: d * DL + (l + 1) * 128],
                                 0, 128, q, extra_w=[("GW", g_wdkv)]))
    kr_done = []
    for half in range(2):
        for q in range(NQ):
            kr_done.append(proj("krx", half * S + q * QB,
                                "wkr_sb", lambda d, h=half: t["wkr_sb"][:, d * DHR + h * 16: d * DHR + h * 16 + 16],
                                0, 16, q, extra_w=[("GW", g_wkr)]))
    qr_done = []
    for half, dst in ((0, "x1q"), (1, "x2q")):
        for c in range(2):
            for q in range(NQ):
                nonloc_pp = pp
                pse = t["ps_e"] if pp == 0 else t["ps_f"]
                w = [("GW", g_wqr)]
                if pb_free[pp] is not None:
                    w.append(("VE", pb_free[pp]))
                for l in range(NL):
                    p.E("tensor", lambda e, pse=pse, l=l, c=c, q=q, h=half: e.matmul(
                        pse[:, :], t["wqr_sb"][:, l * 512 + h * 256 + c * 128: l * 512 + h * 256 + (c + 1) * 128],
                        t["cq"][:, l * S + q * QB: l * S + q * QB + QB],
                        start=(l == 0), stop=(l == NL - 1)),
                        waits=(w + [("VE", cq_done[-1])] if l == 0 else ()),
                        inc=("TE" if l == NL - 1 else None))
                mm = p.cnt["TE"]
                ev = p.E("vector", lambda e, pse=pse, dst=dst, c=c, q=q: e.tensor_copy(
                    t[dst][:, c * S + q * QB: c * S + q * QB + QB], pse[:, :]),
                    waits=[("TE", mm)], inc="VE")
                pb_free[pp] = ev
                qr_done.append(ev)
                pp ^= 1

    def rope_block(x1, x2, co, si, r1, r2, w0, part=128):
        tA = t["tmpa"][0:part, 0:512]
        tB = t["tmpa"][0:part, 512:1024]
        tC = t["sq"].ap() if False else t["sq"]
        tC = t["sq"][0:part, 0:512]
        tD = t["sq"][0:part, 512:1024]
        p.E("vector", lambda e: e.tensor_mul(tA, x1, co), waits=w0, inc="VE")
        p.E("vector", lambda e: e.tensor_mul(tB, x1, si), inc="VE")
        p.E("vector", lambda e: e.tensor_mul(tC, x2, si), inc="VE")
        g = p.E("vector", lambda e: e.tensor_mul(tD, x2, co), inc="VE")
        p.W("vector", [("VE", g)])
        p.E("vector", lambda e: e.tensor_sub(r1, tA, tC), inc="VE")
        return p.E("vector", lambda e: e.tensor_add(r2, tB, tD), inc="VE")

    rope_q_done = []
    for c in range(2):
        for q in range(NQ):
            x1 = t["x1q"][:, c * S + q * QB: c * S + q * QB + QB]
            x2 = t["x2q"][:, c * S + q * QB: c * S + q * QB + QB]
            co = t["cos"][:, q * QB:(q + 1) * QB]
            si = t["sin"][:, q * QB:(q + 1) * QB]
            r1 = t["r12"][:, c * S + q * QB: c * S + q * QB + QB]
            r2 = t["r12"][:, 2 * S + c * S + q * QB: 2 * S + c * S + q * QB + QB]
            w = [("VE", qr_done[-1]), ("LD", ld_misc)] if (c == 0 and q == 0) else ()
            rope_q_done.append(rope_block(x1, x2, co, si, r1, r2, w))

    kr2tmp = t["x1q"]
    rope_k_parts = []
    for q in range(NQ):
        x1 = t["krx"][0:16, q * QB:(q + 1) * QB]
        x2 = t["krx"][0:16, S + q * QB: S + q * QB + QB]
        co = t["cos"][0:16, q * QB:(q + 1) * QB]
        si = t["sin"][0:16, q * QB:(q + 1) * QB]
        r1 = kr2tmp[0:16, q * QB:(q + 1) * QB]
        r2 = kr2tmp[0:16, S + q * QB: S + (q + 1) * QB]
        w = [("VE", rope_q_done[-1]), ("VE", kr_done[-1])] if q == 0 else ()
        rope_k_parts.append(rope_block(x1, x2, co, si, r1, r2, w, part=16))
    v_ropek = rope_k_parts[-1]
    p.E("sync", lambda e: e.dma_start(out=t["krr"][64:80, :], in_=kr2tmp[0:16, 0:S]),
        waits=[("VE", v_ropek)], inc="ST", d=16)
    d_krr = p.E("sync", lambda e: e.dma_start(out=t["krr"][80:96, :], in_=kr2tmp[0:16, S:2 * S]),
                inc="ST", d=16)

    if dbg:
        dump("xn1f", lambda: t["xn1f"][:, :], (128, ND * S), BF16, [("VE", v_xn1f)])
        dump("cq", lambda: t["cq"][:, :], (128, NL * S), BF16, [("VE", cq_done[-1])])
        dump("ckv", lambda: t["ckv"][:, :], (128, NL * S), BF16, [("VE", ckv_done[-1])])
        dump("r12", lambda: t["r12"][:, :], (128, 4 * S), BF16, [("VE", rope_q_done[-1])])
        dump("krr", lambda: t["krr"][64:96, :], (32, S), BF16, [("ST", d_krr)])

    te_projA = p.cnt["TE"]
    if upto in ("B", "C"):
        wB = [("TE", te_projA), ("VE", v_ropek), ("ST", d_krr)]
        for a, b in (("wuq_sb", "uqb"), ("wuk_sb", "ukb"), ("wuv_sb", "uvb")):
            p.E("sync", lambda e, a=a, b=b: e.dma_start(
                out=t[a][:, :].rearrange("p (n m) -> p n m", m=D),
                in_=t[b].rearrange("(n p) m -> p n m", p=128)),
                waits=list(wB) + [("GW", g_uqkv_cast)], inc="LD", d=16)
            wB = []
        g_wuqkv = ("LD", p.cnt["LD"])
        masklo = nc.alloc_sbuf_tensor_at("masklo", [128, 4 * 512], BF16,
                                         offset=SB_BASE + SB["sq"][0])
        t["masklo"] = masklo
        glo = p.E("gpsimd", lambda e: e.memset(masklo[:, :], CLIP_LO),
                  waits=[("VE", v_ropek), ("ST", d_krr)], inc="GP")
        p.W("gpsimd", [("GP", glo)])
        for j in range(4):
            p.E("gpsimd", lambda e, j=j: e.affine_select(
                masklo[:, j * 512:(j + 1) * 512], masklo[:, j * 512:(j + 1) * 512],
                [[1, 512]], ALU.is_ge, 0.0, base=-j * 128, channel_multiplier=-1), inc="GP")
        g_masklo = p.cnt["GP"]
        g_vones = p.E("gpsimd", lambda e: e.memset(
            bass.AP(t["vall"], 64, [[NT * 1040, 128], [1040, NT], [65, H], [1, 1]]), 1.0),
            waits=list(wB), inc="GP")
        for srcn, dstn in (("h1t", "h1b"), ("h2t", "h2b")):
            for oc in range(NOC):
                wv = [("LD", flood_ld_gate)] if (srcn == "h1t" and oc == 0) else ()
                p.E("gpsimd", lambda e, srcn=srcn, dstn=dstn, oc=oc: e.dma_start(
                    out=t[dstn][oc].rearrange("a b -> (a b)"),
                    in_=t[srcn][oc].rearrange("a b -> (a b)")),
                    waits=wv, inc="GF", d=16)
        g_ffncast = p.cnt["GF"]

        vb_free = {0: None, 1: None}
        vpp = 0
        v_vall = None
        for j in range(NT):
            for half in range(2):
                pse = t["ps_e"] if vpp == 0 else t["ps_f"]
                w = [g_wuqkv, ("VE", ckv_done[-1]), ("TE", te_projA)]
                if vb_free[vpp] is not None:
                    w.append(vb_free[vpp])
                for l in range(NL):
                    p.E("tensor", lambda e, pse=pse, l=l, j=j, hf=half: e.matmul(
                        pse[:, :], t["ckv"][:, l * S + j * 128: l * S + (j + 1) * 128],
                        t["wuv_sb"][:, l * D + hf * 512: l * D + hf * 512 + 512],
                        start=(l == 0), stop=(l == NL - 1)),
                        waits=(w if l == 0 else ()), inc=("TE" if l == NL - 1 else None))
                mm = p.cnt["TE"]
                ev = p.E("vector", lambda e, pse=pse, j=j, hf=half: e.tensor_copy(
                    bass.AP(t["vall"], j * 1040 + hf * 520, [[NT * 1040, 128], [65, 8], [1, 64]]),
                    pse[:, :].rearrange("p (h m) -> p h m", m=64)),
                    waits=[("TE", mm)], inc="VE")
                vb_free[vpp] = ("VE", ev)
                vpp ^= 1
        v_vall = p.cnt["VE"]
        if dbg >= 2:
            dump("vallA", lambda: t["vall"][:, 0:2080], (128, 2080), BF16, [("VE", v_vall), ("GP", g_vones)])

        if BARRIER:
            barv = {s: p.cnt[s] for s in ("TE", "VE", "SC", "GP", "LD", "ST", "GW", "LDX")}
            engs = {1: ("tensor", "vector", "scalar", "gpsimd", "sync"),
                    2: ("tensor",), 3: ("sync",), 4: ("vector",), 5: ("scalar",), 6: ("gpsimd",)}[BARRIER]
            for eng in engs:
                p.W(eng, [(s, v) for s, v in barv.items()])

        qcg = 0
        gpi = 0
        ve_stage = {}
        gp_div = {}
        sc_exp = {}
        sc_exp0 = {}
        te_last_score = 0
        for h in range(H):
            for q in range(NQ):
                pse = t["ps_e"] if vpp == 0 else t["ps_f"]
                w = [g_wuqkv, ("VE", cq_done[-1])]
                if vb_free[vpp] is not None:
                    w.append(vb_free[vpp])
                for l in range(NL):
                    p.E("tensor", lambda e, pse=pse, l=l, q=q, h=h: e.matmul(
                        pse[0:64, :], t["wuq_sb"][:, l * D + h * DH: l * D + h * DH + DH],
                        t["cq"][:, l * S + q * QB: l * S + q * QB + QB],
                        start=(l == 0), stop=(l == NL - 1)),
                        waits=(w if l == 0 else ()), inc=("TE" if l == NL - 1 else None))
                mm = p.cnt["TE"]
                ev = p.E("vector", lambda e, pse=pse, q=q: e.tensor_copy(
                    t["qt"][0:64, q * QB:(q + 1) * QB], pse[0:64, :]),
                    waits=[("TE", mm)], inc="VE")
                vb_free[vpp] = ("VE", ev)
                vpp ^= 1
            for q in range(NQ):
                pse = t["ps_e"] if vpp == 0 else t["ps_f"]
                w = [g_wuqkv, ("VE", ckv_done[-1])]
                if vb_free[vpp] is not None:
                    w.append(vb_free[vpp])
                for l in range(NL):
                    p.E("tensor", lambda e, pse=pse, l=l, q=q, h=h: e.matmul(
                        pse[0:64, :], t["wuk_sb"][:, l * D + h * DH: l * D + h * DH + DH],
                        t["ckv"][:, l * S + q * QB: l * S + q * QB + QB],
                        start=(l == 0), stop=(l == NL - 1)),
                        waits=(w if l == 0 else ()), inc=("TE" if l == NL - 1 else None))
                mm = p.cnt["TE"]
                ev = p.E("vector", lambda e, pse=pse, q=q: e.tensor_copy(
                    t["kt"][0:64, q * QB:(q + 1) * QB], pse[0:64, :]),
                    waits=[("TE", mm)], inc="VE")
                vb_free[vpp] = ("VE", ev)
                vpp ^= 1
            v_ktrope = p.E("vector", lambda e: e.tensor_copy(
                t["kt"][64:96, :], t["krr"][64:96, :]),
                waits=[("TE", te_last_score), ("ST", d_krr)], inc="VE")
            rw = [("TE", te_last_score), ("VE", rope_q_done[-1])]
            g_q1 = p.E("sync", lambda e, h=h: e.dma_start(
                out=t["qt"][64:80, :],
                in_=t["r12"][(h % 8) * 16:(h % 8) * 16 + 16, (h // 8) * S:(h // 8) * S + S]),
                waits=rw, inc="RP", d=16)
            g_q2 = p.E("sync", lambda e, h=h: e.dma_start(
                out=t["qt"][80:96, :],
                in_=t["r12"][(h % 8) * 16:(h % 8) * 16 + 16, 2 * S + (h // 8) * S: 2 * S + (h // 8) * S + S]),
                inc="RP", d=16)
            g_rope = p.cnt["RP"]
            v_build = p.cnt["VE"]

            for Q in range(NQ):
                nk = 4 * Q + 4
                npair = nk // 2
                g0 = gpi
                pso = t["ps_c"] if qcg % 2 == 0 else t["ps_d"]
                ve_stt = {}
                pv_t = {}
                if PAIRED:
                    for pr in range(npair + 1):
                        if pr < npair:
                            pss = t["ps_s0"] if gpi % 2 == 0 else t["ps_s1"]
                            te_k20 = None
                            for k2 in range(2):
                                j = 2 * pr + k2
                                diag = j >= nk - 4
                                w = []
                                if j == 0:
                                    w += [("VE", v_build), ("RP", g_rope), ("GP", g_vones)]
                                    if qcg >= 2 and (qcg - 2) in ve_stage:
                                        w.append(("VE", ve_stage[qcg - 2]))
                                if gpi >= 2:
                                    if EXPSPLIT and k2 == 0 and (gpi - 2) in sc_exp0:
                                        w.append(("SC", sc_exp0[gpi - 2]))
                                    elif (EXPSPLIT and k2 == 1) or (not EXPSPLIT and k2 == 0):
                                        w.append(("SC", sc_exp[gpi - 2]))
                                p.E("tensor", lambda e, pss=pss, j=j, Q=Q, diag=diag, k2=k2: e.matmul(
                                    pss[:, k2 * 512:(k2 + 1) * 512],
                                    t["kt"][0:96, j * 128:(j + 1) * 128],
                                    t["qt"][0:96, Q * QB:(Q + 1) * QB],
                                    start=True, stop=(not diag)),
                                    waits=w, inc=(None if diag else "TE"))
                                if diag:
                                    rel = j - (nk - 4)
                                    p.E("tensor", lambda e, pss=pss, rel=rel, k2=k2: e.matmul(
                                        pss[:, k2 * 512:(k2 + 1) * 512], t["ident"][:, :],
                                        t["masks"][:, rel * 512:(rel + 1) * 512],
                                        start=False, stop=True),
                                        waits=[("GP", g_masks)] if (qcg == 0 and rel == 0) else (),
                                        inc="TE")
                                else:
                                    pass
                                if k2 == 0:
                                    te_k20 = p.cnt["TE"]
                            te_pair = p.cnt["TE"]
                            sw = [("TE", te_pair)]
                            if qcg == 0 and pr == 0:
                                sw.append(("GP", g_consts))
                            if EXPSPLIT:
                                sw0 = [("TE", te_k20)]
                                if qcg == 0 and pr == 0:
                                    sw0.append(("GP", g_consts))
                                sc_exp0[gpi] = p.E("scalar", lambda e, gpi=gpi, pss=pss: e.activation(
                                    t["pring"][:, (gpi % 2) * 1024:(gpi % 2) * 1024 + 512],
                                    pss[:, 0:512], AF.Exp, scale=t["consts"][:, 1:2]),
                                    waits=sw0, inc="SC")
                                sc_exp[gpi] = p.E("scalar", lambda e, gpi=gpi, pss=pss: e.activation(
                                    t["pring"][:, (gpi % 2) * 1024 + 512:(gpi % 2) * 1024 + 1024],
                                    pss[:, 512:1024], AF.Exp, scale=t["consts"][:, 1:2]),
                                    waits=[("TE", te_pair)], inc="SC")
                            else:
                                sc_exp[gpi] = p.E("scalar", lambda e, gpi=gpi, pss=pss: e.activation(
                                    t["pring"][:, (gpi % 2) * 1024:(gpi % 2) * 1024 + 1024],
                                    pss[:, :], AF.Exp, scale=t["consts"][:, 1:2]),
                                    waits=sw, inc="SC")
                            vw = [("SC", sc_exp[gpi])]
                            if qcg == 0 and pr == 0:
                                vw.append(("GP", g_masklo))
                            diagpair = (2 * pr) >= nk - 4
                            if diagpair:
                                rel = 2 * pr - (nk - 4)
                                ve_stt[pr] = p.E("vector", lambda e, gpi=gpi, rel=rel: e.scalar_tensor_tensor(
                                    t["pring"][:, (gpi % 2) * 1024:(gpi % 2) * 1024 + 1024],
                                    t["pring"][:, (gpi % 2) * 1024:(gpi % 2) * 1024 + 1024],
                                    CLIP_HI, masklo[:, rel * 512:(rel + 2) * 512],
                                    ALU.min, ALU.max),
                                    waits=vw, inc="VE")
                            else:
                                ve_stt[pr] = p.E("vector", lambda e, gpi=gpi: e.tensor_scalar(
                                    t["pring"][:, (gpi % 2) * 1024:(gpi % 2) * 1024 + 1024],
                                    t["pring"][:, (gpi % 2) * 1024:(gpi % 2) * 1024 + 1024],
                                    CLIP_HI, CLIP_LO, ALU.min, ALU.max),
                                    waits=vw, inc="VE")
                            gpi += 1
                        if pr >= 1:
                            for k2 in range(2):
                                jj = 2 * (pr - 1) + k2
                                slotb = (g0 + pr - 1) % 2
                                p.E("tensor", lambda e, jj=jj, h=h, pso=pso, k2=k2, slotb=slotb: e.matmul(
                                    pso[0:65, :], t["vall"][:, jj * 1040 + h * 65: jj * 1040 + h * 65 + 65],
                                    t["pring"][:, slotb * 1024 + k2 * 512: slotb * 1024 + k2 * 512 + 512],
                                    start=(jj == 0), stop=(jj == nk - 1)),
                                    waits=([("VE", ve_stt[pr - 1])] if k2 == 0 else ()), inc="TE")
                                pv_t[jj] = p.cnt["TE"]
                else:
                    te_score = {}
                    for j in range(nk + 1):
                        if j < nk:
                            diag = j >= nk - 4
                            pss = t["ps_s0"][:, 0:512] if j % 2 == 0 else t["ps_s1"][:, 0:512]
                            w = []
                            if j == 0:
                                w += [("VE", v_build), ("RP", g_rope), ("GP", g_vones)]
                                if qcg >= 2 and (qcg - 2) in ve_stage:
                                    w.append(("VE", ve_stage[qcg - 2]))
                            if k := (gpi >= 2 and j == j):
                                pass
                            if gpi >= 2:
                                w.append(("SC", sc_exp[gpi - 2]))
                            p.E("tensor", lambda e, pss=pss, j=j, Q=Q, diag=diag: e.matmul(
                                pss, t["kt"][0:96, j * 128:(j + 1) * 128],
                                t["qt"][0:96, Q * QB:(Q + 1) * QB],
                                start=True, stop=(not diag)),
                                waits=w, inc=(None if diag else "TE"))
                            if diag:
                                rel = j - (nk - 4)
                                p.E("tensor", lambda e, pss=pss, rel=rel: e.matmul(
                                    pss, t["ident"][:, :],
                                    t["masks"][:, rel * 512:(rel + 1) * 512],
                                    start=False, stop=True),
                                    waits=[("GP", g_masks)] if (qcg == 0 and rel == 0) else (),
                                    inc="TE")
                            te_score[j] = p.cnt["TE"]
                            sw = [("TE", te_score[j])]
                            if qcg == 0 and j == 0:
                                sw.append(("GP", g_consts))
                            sc_exp[gpi] = p.E("scalar", lambda e, gpi=gpi, pss=pss: e.activation(
                                t["pring"][:, (gpi % 4) * 512:(gpi % 4) * 512 + 512],
                                pss, AF.Exp, scale=t["consts"][:, 1:2]),
                                waits=sw, inc="SC")
                            vw = [("SC", sc_exp[gpi])]
                            if qcg == 0 and j == 0:
                                vw.append(("GP", g_masklo))
                            if diag:
                                rel = j - (nk - 4)
                                ve_stt[j] = p.E("vector", lambda e, gpi=gpi, rel=rel: e.scalar_tensor_tensor(
                                    t["pring"][:, (gpi % 4) * 512:(gpi % 4) * 512 + 512],
                                    t["pring"][:, (gpi % 4) * 512:(gpi % 4) * 512 + 512],
                                    CLIP_HI, masklo[:, rel * 512:(rel + 1) * 512],
                                    ALU.min, ALU.max),
                                    waits=vw, inc="VE")
                            else:
                                ve_stt[j] = p.E("vector", lambda e, gpi=gpi: e.tensor_scalar(
                                    t["pring"][:, (gpi % 4) * 512:(gpi % 4) * 512 + 512],
                                    t["pring"][:, (gpi % 4) * 512:(gpi % 4) * 512 + 512],
                                    CLIP_HI, CLIP_LO, ALU.min, ALU.max),
                                    waits=vw, inc="VE")
                            gpi += 1
                        if j >= 1:
                            jj = j - 1
                            slotb = (g0 + jj) % 4
                            p.E("tensor", lambda e, jj=jj, h=h, pso=pso, slotb=slotb: e.matmul(
                                pso[0:65, :], t["vall"][:, jj * 1040 + h * 65: jj * 1040 + h * 65 + 65],
                                t["pring"][:, slotb * 512: slotb * 512 + 512],
                                start=(jj == 0), stop=(jj == nk - 1)),
                                waits=[("VE", ve_stt[jj])], inc="TE")
                            pv_t[jj] = p.cnt["TE"]
                te_last_score = p.cnt["TE"]
                sv = [("TE", pv_t[nk - 1])]
                if qcg >= 1 and (qcg - 1) in gp_div:
                    sv.append(("GP", gp_div[qcg - 1]))
                ve_stage[qcg] = p.E("vector", lambda e, pso=pso: e.tensor_copy(
                    t["stage"][0:65, :], pso[0:65, :]), waits=sv, inc="VE")
                psb = t["ps_e"] if qcg % 2 == 0 else t["ps_f"]
                bw = [("VE", ve_stage[qcg]), ("GP", g_onesf)]
                if vb_free[0 if qcg % 2 == 0 else 1] is not None:
                    bw.append(vb_free[0 if qcg % 2 == 0 else 1])
                mmb = p.E("tensor", lambda e, psb=psb: e.matmul(
                    psb[0:64, :], t["onesf"][64:65, 0:64], t["stage"][64:65, :],
                    start=True, stop=True), waits=bw, inc="TE")
                sln = p.E("scalar", lambda e, psb=psb: e.activation(
                    t["rb"][0:64, :], psb[0:64, :], AF.Ln),
                    waits=[("TE", mmb)] + ([("GP", gp_div[qcg - 1])] if (qcg - 1) in gp_div else []),
                    inc="SC")
                p.W("scalar", [("SC", sln)])
                srcp = p.E("scalar", lambda e: e.activation(
                    t["rb"][0:64, :], t["rb"][0:64, :], AF.Exp,
                    scale=t["consts"][0:64, 2:3]), inc="SC")
                vb_free[0 if qcg % 2 == 0 else 1] = ("SC", sln)
                gp_div[qcg] = p.E("gpsimd", lambda e, h=h, Q=Q: e.tensor_tensor(
                    t["xn1f"][(h % 2) * 64:(h % 2) * 64 + 64,
                              (h // 2) * S + Q * QB:(h // 2) * S + (Q + 1) * QB],
                    t["stage"][0:64, :], t["rb"][0:64, :], ALU.mult),
                    waits=[("SC", srcp)], inc="GP")
                if dbg == 2 and h == 0 and Q == 0:
                    dump("vall", lambda: t["vall"][:, :], (128, NT * 1040), BF16, [("VE", v_vall), ("GP", g_vones)])
                    dump("qt0", lambda: t["qt"][0:96, :], (96, S), BF16, [("VE", v_build), ("RP", g_rope)])
                    dump("kt0", lambda: t["kt"][0:96, :], (96, S), BF16, [("VE", v_build)])
                    dump("p0", lambda: t["pring"][:, :], (128, 4 * 512), BF16, [("VE", ve_stt[3])])
                    dump("stage0", lambda: t["stage"][0:65, :], (65, 512), F32, [("VE", ve_stage[qcg])])
                    dump("rb0", lambda: t["rb"][0:64, :], (64, 512), F32, [("SC", srcp)])
                qcg += 1
        gp_attn = p.cnt["GP"]

        p.E("sync", lambda e: e.dma_start(
            out=t["wo_sb"][:, :].rearrange("p (n m) -> p n m", m=D),
            in_=t["wob"].rearrange("(n p) m -> p n m", p=128)),
            waits=[("TE", te_last_score), ("VE", p.cnt["VE"]), ("GW", g_uqkv_cast)],
            inc="LD", d=16)
        g_wo = ("LD", p.cnt["LD"])
        ps_free = {0: None, 1: None}
        yt_free = {0: None, 1: None}
        for n in range(NT):
            ypp = n % 2
            yt = t["ytile"][:, ypp * D:(ypp + 1) * D]
            mms = []
            for half in range(2):
                pse = t["ps_e"] if half == 0 else t["ps_f"]
                w = [g_wo, ("GP", gp_attn)]
                if ps_free[half] is not None:
                    w.append(("VE", ps_free[half]))
                for a in range(ND):
                    p.E("tensor", lambda e, pse=pse, a=a, n=n, hf=half: e.matmul(
                        pse[:, :], t["xn1f"][:, a * S + n * 128: a * S + (n + 1) * 128],
                        t["wo_sb"][:, a * D + hf * 512: a * D + hf * 512 + 512],
                        start=(a == 0), stop=(a == ND - 1)),
                        waits=(w if a == 0 else ()), inc=("TE" if a == ND - 1 else None))
                mms.append(p.cnt["TE"])
            ew = [] if yt_free[ypp] is None else [("VE", yt_free[ypp])]
            e0 = p.E("vector", lambda e, yt=yt: e.tensor_copy(
                yt[:, 0:512], t["ps_e"][:, :]), waits=[("TE", mms[0])] + ew, inc="VE")
            e1 = p.E("vector", lambda e, yt=yt: e.tensor_copy(
                yt[:, 512:1024], t["ps_f"][:, :]), waits=[("TE", mms[1])], inc="VE")
            ps_free[0], ps_free[1] = e0, e1
            p.W("vector", [("VE", e1)])
            vsum = p.E("vector", lambda e, yt=yt, n=n: e.reduce_sum(
                st[:, n:n + 1], yt, axis=mybir.AxisListType.X), inc="VE")
            vsq = p.E("vector", lambda e, yt=yt, n=n: e.scalar_tensor_tensor(
                t["sq"][:, :], yt, 1.0, yt, ALU.mult, ALU.mult,
                accum_out=st[:, 16 + n:17 + n]), inc="VE")
            p.W("vector", [("VE", vsq)])
            p.E("vector", lambda e, n=n: e.tensor_scalar_mul(st[:, 32 + n:33 + n], st[:, n:n + 1], 1.0 / D), inc="VE")
            vmsq = p.E("vector", lambda e, n=n: e.tensor_scalar_mul(st[:, 64 + n:65 + n], st[:, 16 + n:17 + n], 1.0 / D), inc="VE")
            smsq = p.E("scalar", lambda e, n=n: e.activation(st[:, 80 + n:81 + n], st[:, 32 + n:33 + n], AF.Square),
                       waits=[("VE", vmsq)], inc="SC")
            vvar = p.E("vector", lambda e, n=n: e.tensor_sub(st[:, 96 + n:97 + n], st[:, 64 + n:65 + n], st[:, 80 + n:81 + n]),
                       waits=[("SC", smsq)], inc="VE")
            slnv = p.E("scalar", lambda e, n=n: e.activation(st[:, 112 + n:113 + n], st[:, 96 + n:97 + n], AF.Ln,
                                                             bias=t["consts"][:, 0:1]),
                       waits=[("VE", vvar)], inc="SC")
            p.W("scalar", [("SC", slnv)])
            sinv = p.E("scalar", lambda e, n=n: e.activation(st[:, 48 + n:49 + n], st[:, 112 + n:113 + n], AF.Exp,
                                                             scale=t["consts"][:, 3:4]), inc="SC")
            p.W("vector", [("SC", sinv)])
            vap = p.E("vector", lambda e, yt=yt, n=n: e.tensor_scalar(
                t["sq"][:, :], yt, st[:, 32 + n:33 + n], st[:, 48 + n:49 + n],
                ALU.subtract, ALU.mult), inc="VE")
            p.W("vector", [("VE", vap)])
            vadd = p.E("vector", lambda e, n=n: e.tensor_add(
                t["xres"][:, n * D:(n + 1) * D], t["xres"][:, n * D:(n + 1) * D],
                t["sq"][:, :]), inc="VE")
            yt_free[ypp] = vap
        v_x1 = p.cnt["VE"]

        if dbg:
            dump("attn", lambda: t["xn1f"][:, :], (128, ND * S), BF16, [("GP", gp_attn)])
        if dbg or dbg == 3:
            pass
        if dbg == 3 or dbg == 1 or dbg == 2:
            dump("x1", lambda: t["xres"][:, :], (128, NT * D), F32, [("VE", v_x1)])
        if dbg == 3:
            dump("mlo", lambda: masklo[:, :], (128, 2048), BF16, [("VE", v_x1)])
            dump("mbs", lambda: t["masks"][:, :], (128, 2048), BF16, [("VE", v_x1)])

    if upto == "C":
        st3 = t["stats"]
        te_c0 = p.cnt["TE"]
        sc_b = p.cnt["SC"]
        ve_b = p.cnt["VE"]

        ld_w3r = p.E("sync", lambda e: e.dma_start(
            out=t["w3srow"][64:65, :], in_=t["w3s_row"][:, :]),
            waits=[("VE", v_x1)], inc="LD", d=16)
        mm3a = p.E("tensor", lambda e: e.matmul(
            t["ps_c"][:, :], t["onesf"][64:65, 0:128], t["w3srow"][64:65, 0:512],
            start=True, stop=True),
            waits=[("LD", ld_w3r), ("GP", g_onesf), ("SC", sc_b), ("VE", ve_b)], inc="TE")
        mm3b = p.E("tensor", lambda e: e.matmul(
            t["ps_d"][:, :], t["onesf"][64:65, 0:128], t["w3srow"][64:65, 512:1024],
            start=True, stop=True), inc="TE")
        v3a = p.E("vector", lambda e: e.tensor_copy(t["w3sb"][:, 0:512], t["ps_c"][:, :]),
                  waits=[("TE", mm3a)], inc="VE")
        v3b = p.E("vector", lambda e: e.tensor_copy(t["w3sb"][:, 512:1024], t["ps_d"][:, :]),
                  waits=[("TE", mm3b)], inc="VE")

        def ln_chain(src_ap, n, waits0=()):
            vsum = p.E("vector", lambda e: e.reduce_sum(
                st3[:, n:n + 1], src_ap, axis=mybir.AxisListType.X), waits=waits0, inc="VE")
            vsq = p.E("vector", lambda e: e.scalar_tensor_tensor(
                t["sqc"][:, :], src_ap, 1.0, src_ap, ALU.mult, ALU.mult,
                accum_out=st3[:, 16 + n:17 + n]), inc="VE")
            p.W("vector", [("VE", vsq)])
            p.E("vector", lambda e: e.tensor_scalar_mul(st3[:, 32 + n:33 + n], st3[:, n:n + 1], 1.0 / D), inc="VE")
            vmsq = p.E("vector", lambda e: e.tensor_scalar_mul(st3[:, 64 + n:65 + n], st3[:, 16 + n:17 + n], 1.0 / D), inc="VE")
            smsq = p.E("scalar", lambda e: e.activation(st3[:, 80 + n:81 + n], st3[:, 32 + n:33 + n], AF.Square),
                       waits=[("VE", vmsq)], inc="SC")
            vvar = p.E("vector", lambda e: e.tensor_sub(st3[:, 96 + n:97 + n], st3[:, 64 + n:65 + n], st3[:, 80 + n:81 + n]),
                       waits=[("SC", smsq)], inc="VE")
            slnv = p.E("scalar", lambda e: e.activation(st3[:, 112 + n:113 + n], st3[:, 96 + n:97 + n], AF.Ln,
                                                        bias=t["consts"][:, 0:1]),
                       waits=[("VE", vvar)], inc="SC")
            p.W("scalar", [("SC", slnv)])
            return p.E("scalar", lambda e: e.activation(st3[:, 48 + n:49 + n], st3[:, 112 + n:113 + n], AF.Exp,
                                                        scale=t["consts"][:, 3:4]), inc="SC")

        mm_w12 = {}
        mm_h = {}
        mm_w3slot = {}
        ps_ef_free = {0: ("VE", ve_b), 1: ("VE", ve_b)}
        ps_q_free = {0: ("SC", sc_b), 1: ("SC", sc_b), 2: ("VE", v3a), 3: ("VE", v3b)}
        first_c_load = True

        ln3_vap = {}

        def emit_ln3(Tn):
            for sub in range(4):
                n = Tn * 4 + sub
                x1t = t["xres"][:, n * D:(n + 1) * D]
                sinv = ln_chain(x1t, n)
                p.W("vector", [("SC", sinv)])
                ln3_vap[(Tn, sub)] = p.E("vector", lambda e, x1t=x1t, n=n, sub=sub: e.tensor_scalar(
                    t["xn2t"][:, sub * D:(sub + 1) * D], x1t,
                    st3[:, 32 + n:33 + n], st3[:, 48 + n:49 + n],
                    ALU.subtract, ALU.mult), inc="VE")

        emit_ln3(0)
        for T in range(NTFF):
            tr_ev = {}
            for sub in range(4):
                vap = ln3_vap[(T, sub)]
                for half in range(2):
                    pst = t["ps_e"] if half == 0 else t["ps_f"]
                    w = [("VE", vap), ps_ef_free[half]]
                    for k in range(4):
                        d = half * 4 + k
                        p.E("tensor", lambda e, pst=pst, d=d, k=k, sub=sub: e.matmul(
                            pst[:, k * 128:(k + 1) * 128],
                            t["xn2t"][:, sub * D + d * 128: sub * D + (d + 1) * 128],
                            t["ident"][:, :], start=True, stop=True),
                            waits=(w if k == 0 else ()), inc=("TE" if k == 3 else None))
                    mmt = p.cnt["TE"]
                    tr_ev[(sub, half)] = p.E("vector", lambda e, pst=pst, sub=sub, half=half: e.tensor_copy(
                        bass.AP(t["xn2f"], half * 4 * TFF + sub * 128, [[ND * TFF, 128], [TFF, 4], [1, 128]]),
                        pst[:, :].rearrange("p (d m) -> p d m", m=128)),
                        waits=[("TE", mmt)], inc="VE")
                    ps_ef_free[half] = ("VE", tr_ev[(sub, half)])
            v_xn2f = p.cnt["VE"]

            if dbg and T == 0:
                dump("xn2f", lambda: t["xn2f"][:, :], (128, ND * TFF), BF16, [("VE", v_xn2f)])

            def xn2f_sl(d):
                return t["xn2f"][:, d * TFF:(d + 1) * TFF]

            ld_w12 = {}

            def emit_w12_load(oc):
                if oc >= NOC:
                    return
                sem = "W12A" if oc % 2 == 0 else "W12B"
                wv = []
                if T == 0 and oc < 2:
                    wv.append(("GF", g_ffncast))
                if first_c[0]:
                    wv.append(("TE", te_c0))
                    first_c[0] = False
                key = (T, oc - 2) if oc >= 2 else (T - 1, NOC - 2 + oc)
                if key in mm_w12:
                    wv.append(("TE", mm_w12[key]))
                p.E("gpsimd", lambda e, oc=oc: e.dma_start(
                    out=bass.AP(t["w12buf"], (oc % 2) * 1024, [[4096, 128], [128, 8], [1, 128]]),
                    in_=t["w1b"][oc].rearrange("(ic p) o -> p ic o", p=128)),
                    waits=wv, inc=sem, d=16)
                ld_w12[oc] = p.E("gpsimd", lambda e, oc=oc: e.dma_start(
                    out=bass.AP(t["w12buf"], 2048 + (oc % 2) * 1024, [[4096, 128], [128, 8], [1, 128]]),
                    in_=t["w2b"][oc].rearrange("(ic p) o -> p ic o", p=128)),
                    inc=sem, d=16)

            first_c = [first_c_load]
            first_c_load = False
            emit_w12_load(0)
            emit_w12_load(1)

            v_a = None
            for oc in range(NOC):
                sem = "W12A" if oc % 2 == 0 else "W12B"
                pse = t["ps_e"] if oc % 2 == 0 else t["ps_f"]
                w = [(sem, ld_w12[oc]), ("VE", v_xn2f), ps_ef_free[oc % 2]]
                for ic in range(ND):
                    p.E("tensor", lambda e, pse=pse, ic=ic, oc=oc: e.matmul(
                        pse[:, :], t["w12buf"][:, (oc % 2) * 1024 + ic * 128:(oc % 2) * 1024 + (ic + 1) * 128],
                        xn2f_sl(ic), start=(ic == 0), stop=(ic == ND - 1)),
                        waits=(w if ic == 0 else ()), inc=("TE" if ic == ND - 1 else None))
                m1 = p.cnt["TE"]
                sw = [("TE", m1)]
                if dbg and T == 0 and oc == 0:
                    dva = p.E("vector", lambda e, pse=pse: e.tensor_copy(
                        t["sqc"][:, 0:512], pse[:, :]), waits=[("TE", m1)], inc="VE")
                    dump("t1raw", lambda: t["sqc"][:, 0:512], (128, 512), F32, [("VE", dva)])
                    sw.append(("VE", dva))
                if v_a is not None:
                    sw.append(("VE", v_a))
                sgate = p.E("scalar", lambda e, pse=pse, oc=oc: e.activation(
                    t["b_sb"][:, 0:512], pse[:, :], AF.Silu, scale=t["w1s_sb"][:, oc:oc + 1]),
                    waits=sw, inc="SC")
                for ic in range(ND):
                    p.E("tensor", lambda e, pse=pse, ic=ic, oc=oc: e.matmul(
                        pse[:, :], t["w12buf"][:, 2048 + (oc % 2) * 1024 + ic * 128: 2048 + (oc % 2) * 1024 + (ic + 1) * 128],
                        xn2f_sl(ic), start=(ic == 0), stop=(ic == ND - 1)),
                        waits=([("SC", sgate)] if ic == 0 else ()), inc=("TE" if ic == ND - 1 else None))
                m2 = p.cnt["TE"]
                mm_w12[(T, oc)] = m2
                emit_w12_load(oc + 2)
                vup = p.E("vector", lambda e, pse=pse, oc=oc: e.tensor_scalar_mul(
                    t["b_sb"][:, 512:1024], pse[:, :], t["w2s_sb"][:, oc:oc + 1]),
                    waits=[("TE", m2)], inc="VE")
                ps_ef_free[oc % 2] = ("VE", vup)
                p.W("vector", [("VE", vup)])
                v_a = p.E("vector", lambda e, oc=oc: e.tensor_mul(
                    t["a_sb"][:, oc * TFF:(oc + 1) * TFF], t["b_sb"][:, 0:512],
                    t["b_sb"][:, 512:1024]),
                    waits=[("SC", sgate)], inc="VE")

            if dbg and T == 0:
                dump("asb", lambda: t["a_sb"][:, 0:4096], (128, 4096), BF16, [("VE", v_a)])

            src_ready = ("VE", v_a)
            for stage, (hb, hs, srcb, dstb) in enumerate((
                    ("h1b", "h1s_sb", "a_sb", "b_sb"),
                    ("h2b", "h2s_sb", "b_sb", "a_sb"))):
                ld_h = {}

                def emit_h_load(oc, hb=hb, stage=stage, ld_h_ref=None):
                    if oc >= NOC:
                        return
                    sem = "HBA" if oc % 2 == 0 else "HBB"
                    wv = []
                    key = (T, stage, oc - 2) if oc >= 2 else (
                        (T, stage - 1, NOC - 2 + oc) if stage > 0 else
                        ((T - 1, 1, NOC - 2 + oc) if T > 0 else None))
                    if key in mm_h:
                        wv.append(("TE", mm_h[key]))
                    elif T == 0 and stage == 0:
                        wv.append(("TE", te_c0))
                        wv.append(("GF", g_ffncast))
                    ld_h[oc] = p.E("gpsimd", lambda e, oc=oc, hb=hb: e.dma_start(
                        out=bass.AP(t["hbuf"], (oc % 2) * 4096, [[8192, 128], [128, 32], [1, 128]]),
                        in_=t[hb][oc].rearrange("(ic p) o -> p ic o", p=128)),
                        waits=wv, inc=sem, d=16)

                emit_h_load(0)
                emit_h_load(1)
                for oc in range(NOC):
                    sem = "HBA" if oc % 2 == 0 else "HBB"
                    pse = t["ps_e"] if oc % 2 == 0 else t["ps_f"]
                    w = [(sem, ld_h[oc]), ps_ef_free[oc % 2]]
                    if oc <= 1:
                        w.append(src_ready)
                    for ic in range(NOC):
                        p.E("tensor", lambda e, pse=pse, ic=ic, oc=oc, srcb=srcb: e.matmul(
                            pse[:, :], t["hbuf"][:, (oc % 2) * 4096 + ic * 128:(oc % 2) * 4096 + (ic + 1) * 128],
                            t[srcb][:, ic * TFF:(ic + 1) * TFF],
                            start=(ic == 0), stop=(ic == NOC - 1)),
                            waits=(w if ic == 0 else ()), inc=("TE" if ic == NOC - 1 else None))
                    mm_h[(T, stage, oc)] = p.cnt["TE"]
                    emit_h_load(oc + 2)
                    sg = p.E("scalar", lambda e, pse=pse, oc=oc, hs=hs, dstb=dstb: e.activation(
                        t[dstb][:, oc * TFF:(oc + 1) * TFF], pse[:, :], AF.Gelu,
                        scale=t[hs][:, oc:oc + 1]),
                        waits=[("TE", mm_h[(T, stage, oc)])], inc="SC")
                    ps_ef_free[oc % 2] = ("SC", sg)
                src_ready = ("SC", p.cnt["SC"])
                if dbg and T == 0 and stage == 0:
                    dump("bsb", lambda: t["b_sb"][:, 0:4096], (128, 4096), BF16, [src_ready])
                if dbg and T == 0 and stage == 1:
                    dump("csb", lambda: t["a_sb"][:, 0:4096], (128, 4096), BF16, [src_ready])
            c_done = src_ready
            if T + 1 < NTFF:
                emit_ln3(T + 1)

            psq = [(t["ps_s0"], 0), (t["ps_s0"], 512), (t["ps_s1"], 0), (t["ps_s1"], 512)]
            for pas in range(2):
                ld_w3 = {}

                def emit_w3_load(ic, pas=pas, ld_w3_ref=None):
                    if ic >= NOC:
                        return
                    slot = ic % 6
                    sem = "W3%d" % slot
                    prev_in_prior = slot + 30 if slot < 2 else slot + 24
                    wv = []
                    if (T, pas, ic - 6) in mm_w3slot:
                        wv.append(("TE", mm_w3slot[(T, pas, ic - 6)]))
                    elif pas > 0 and (T, pas - 1, prev_in_prior) in mm_w3slot:
                        wv.append(("TE", mm_w3slot[(T, pas - 1, prev_in_prior)]))
                    elif T > 0 and (T - 1, 1, prev_in_prior) in mm_w3slot:
                        wv.append(("TE", mm_w3slot[(T - 1, 1, prev_in_prior)]))
                    if not wv and T == 0 and pas == 0:
                        wv.append(("TE", te_c0))
                        wv.append(("GF", g_ffncast))
                    ld_w3[ic] = (sem, p.E("gpsimd", lambda e, ic=ic, slot=slot: e.dma_start(
                        out=t["w3buf"][:, slot * D:(slot + 1) * D], in_=t["w3b"][ic]),
                        waits=wv, inc=sem, d=16))

                for ic in range(6):
                    emit_w3_load(ic)
                for ic in range(NOC):
                    slot = ic % 6
                    for qi in range(4):
                        subL, oh = 2 * pas + qi // 2, qi % 2
                        ps = psq[qi]
                        w = []
                        if qi == 0:
                            w.append(ld_w3[ic])
                        if ic == 0:
                            w.append(c_done)
                            w.append(ps_q_free[qi])
                        p.E("tensor", lambda e, ps=ps, ic=ic, subL=subL, oh=oh, slot=slot: e.matmul(
                            ps[0][:, ps[1]:ps[1] + 512],
                            t["a_sb"][:, ic * TFF + subL * 128: ic * TFF + subL * 128 + 128],
                            t["w3buf"][:, slot * D + oh * 512: slot * D + oh * 512 + 512],
                            start=(ic == 0), stop=(ic == NOC - 1)),
                            waits=[x for x in w if x], inc=("TE" if qi == 3 else None))
                    mm_w3slot[(T, pas, ic)] = p.cnt["TE"]
                    emit_w3_load(ic + 6)
                mm_last = p.cnt["TE"]
                for qi2 in range(2):
                    subL = 2 * pas + qi2
                    n = T * 4 + subL
                    ev0 = p.E("vector", lambda e, qi2=qi2: e.tensor_tensor(
                        t["ztile"][:, 0:512], psq[qi2 * 2][0][:, psq[qi2 * 2][1]:psq[qi2 * 2][1] + 512],
                        t["w3sb"][:, 0:512], ALU.mult),
                        waits=[("TE", mm_last)], inc="VE")
                    ev1 = p.E("vector", lambda e, qi2=qi2: e.tensor_tensor(
                        t["ztile"][:, 512:1024], psq[qi2 * 2 + 1][0][:, psq[qi2 * 2 + 1][1]:psq[qi2 * 2 + 1][1] + 512],
                        t["w3sb"][:, 512:1024], ALU.mult),
                        inc="VE")
                    ps_q_free[qi2 * 2] = ("VE", ev0)
                    ps_q_free[qi2 * 2 + 1] = ("VE", ev1)
                    p.W("vector", [("VE", ev1)])
                    if dbg and T == 0 and pas == 0 and qi2 == 0:
                        dump("zt0", lambda: t["ztile"][:, :], (128, D), F32, [("VE", ev1)])
                    if dbg and T == 1 and pas == 0 and qi2 == 0:
                        dva2 = p.E("vector", lambda e: e.tensor_copy(
                            t["outt"][:, :], t["ztile"][:, :]), waits=[("VE", ev1)], inc="VE")
                        dump("zt4", lambda: t["outt"][:, :], (128, D), F32, [("VE", dva2)])
                        dump("xn2fT1", lambda: t["xn2f"][:, :], (128, ND * TFF), BF16, [("VE", v_xn2f)])
                        dump("csbT1", lambda: t["a_sb"][:, 0:4096], (128, 4096), BF16, [c_done])
                    sinv = ln_chain(t["ztile"][:, :], n)
                    p.W("vector", [("SC", sinv)])
                    vap = p.E("vector", lambda e, n=n: e.tensor_scalar(
                        t["sqc"][:, :], t["ztile"][:, :],
                        st3[:, 32 + n:33 + n], st3[:, 48 + n:49 + n],
                        ALU.subtract, ALU.mult), inc="VE")
                    p.W("vector", [("VE", vap)])
                    vadd = p.E("vector", lambda e, n=n: e.tensor_add(
                        t["xres"][:, n * D:(n + 1) * D], t["xres"][:, n * D:(n + 1) * D],
                        t["sqc"][:, :]), inc="VE")
                    p.E("gpsimd", lambda e, n=n: e.dma_start(
                        out=t["y"][n * 128:(n + 1) * 128, :],
                        in_=t["xres"][:, n * D:(n + 1) * D]),
                        waits=[("VE", vadd)], inc="ST", d=16)

    p.W("sync", [("ST", p.cnt["ST"])])
    p.W("gpsimd", [("GW", p.cnt["GW"]), ("GF", p.cnt["GF"])])

    p.replay()
    for g in reversed(pg):
        g.__exit__(None, None, None)
    return nc, dbg_outs


def _host_inputs(inp, core):
    f = np.ascontiguousarray
    W_QR = np.asarray(inp["L0_attn_W_QR"])
    W_KR = np.asarray(inp["L0_attn_W_KR"])
    d = {
        "x": f(np.asarray(inp["L0_attn_x"])[core]),
        "wdq": f(np.asarray(inp["L0_attn_W_DQ"])),
        "wdkv": f(np.asarray(inp["L0_attn_W_DKV"])),
        "wuq": f(np.asarray(inp["L0_attn_W_UQ"])),
        "wuk": f(np.asarray(inp["L0_attn_W_UK"])),
        "wuv": f(np.asarray(inp["L0_attn_W_UV"])),
        "wqr": f(np.concatenate([W_QR[:, 0::2], W_QR[:, 1::2]], axis=1)),
        "wkr": f(np.concatenate([W_KR[:, 0::2], W_KR[:, 1::2]], axis=1)),
        "wo": f(np.asarray(inp["L0_attn_W_O"])),
        "w1t": f(np.asarray(inp["L0_ffn_w1_w"]).T.reshape(D, NOC, 128).transpose(1, 0, 2)),
        "w2t": f(np.asarray(inp["L0_ffn_w2_w"]).T.reshape(D, NOC, 128).transpose(1, 0, 2)),
        "h1t": f(np.asarray(inp["L0_ffn_h1_w"]).T.reshape(DHID, NOC, 128).transpose(1, 0, 2)),
        "h2t": f(np.asarray(inp["L0_ffn_h2_w"]).T.reshape(DHID, NOC, 128).transpose(1, 0, 2)),
        "w3t": f(np.asarray(inp["L0_ffn_w3_w"]).T.reshape(NOC, 128, D)),
        "w1s": f(np.asarray(inp["L0_ffn_w1_s"]).reshape(NOC, 128).T),
        "w2s": f(np.asarray(inp["L0_ffn_w2_s"]).reshape(NOC, 128).T),
        "h1s": f(np.asarray(inp["L0_ffn_h1_s"]).reshape(NOC, 128).T),
        "h2s": f(np.asarray(inp["L0_ffn_h2_s"]).reshape(NOC, 128).T),
        "w3s": f(np.asarray(inp["L0_ffn_w3_s"]).reshape(ND, 128).T),
        "w3s_row": f(np.asarray(inp["L0_ffn_w3_s"]).reshape(1, D)),
    }
    fidx = np.arange(128) % 16
    inv = (10000.0 ** (-(2.0 * fidx) / DHR)).astype(np.float64)
    ang = np.arange(S, dtype=np.float64)[None, :] * inv[:, None]
    d["costab"] = np.cos(ang).astype(np.float32)
    d["sintab"] = np.sin(ang).astype(np.float32)
    return d


def kernel(**inputs):
    nc, _ = build_program(dbg=0)
    in_maps = [_host_inputs(inputs, c) for c in range(8)]
    res = run_bass_kernel_spmd(nc, in_maps, core_ids=list(range(8)))
    return np.stack([np.asarray(res.results[c]["y"]) for c in range(8)], axis=0)
